# version 23
# baseline (speedup 1.0000x reference)
import numpy as np
import ml_dtypes
import concourse.bass as bass
import concourse.mybir as mybir
from concourse.bass_utils import run_bass_kernel_spmd

F32 = mybir.dt.float32
BF16 = mybir.dt.bfloat16
AF = mybir.ActivationFunctionType
ALU = mybir.AluOpType

ENGINES = ("tensor", "vector", "scalar", "gpsimd", "sync")

C = 1024
TOK = 4096
TW = 512
NT = TOK // TW
NJ = TW // 128
NS = NT * NJ
LN_EPS = 1e-5
ATTN_EPS = 1e-6


class Prog:
    def __init__(self, nc):
        self.nc = nc
        self.sems = {}
        self.counts = {}
        self.stream = {e: [] for e in ENGINES}

    def sem(self, name):
        if name not in self.sems:
            self.sems[name] = self.nc.alloc_semaphore(name=name)
            self.counts[name] = 0
        return self.sems[name]

    def val(self, name):
        return self.counts.get(name, 0)

    def op(self, eng, builder, waits=(), inc=None, inc_n=1):
        wl = {}
        for w in waits:
            if w is None:
                continue
            (sname, v) = w
            if v > 0:
                self.sem(sname)
                wl[sname] = max(wl.get(sname, 0), v)
        if inc is not None:
            self.sem(inc)
            self.counts[inc] += inc_n
        self.stream[eng].append((builder, sorted(wl.items()), inc, inc_n))
        return (inc, self.counts[inc]) if inc is not None else None

    def emit(self, block):
        def mk(eng_name):
            def body(eng):
                for builder, waits, inc, inc_n in self.stream[eng_name]:
                    for (sname, v) in waits:
                        eng.wait_ge(self.sems[sname], v)
                    if builder is not None:
                        inst = builder(eng)
                        if inc is not None:
                            if inst is None:
                                inst = eng.nop()
                            inst.then_inc(self.sems[inc], inc_n)
            return body

        for e in ENGINES:
            if self.stream[e]:
                getattr(block, e)(mk(e))


def build_program(dbg=False):
    nc = bass.Bass("TRN2", target_bir_lowering=False, debug=False)

    x_hbm = nc.dram_tensor("x", [TOK, C], F32, kind="ExternalInput").ap()
    wqkvT = nc.dram_tensor("wqkvT", [C, 3 * C], F32, kind="ExternalInput").ap()
    wprojT = nc.dram_tensor("wprojT", [C, C], F32, kind="ExternalInput").ap()
    wffnT = nc.dram_tensor("wffnT", [C, C], F32, kind="ExternalInput").ap()
    bproj = nc.dram_tensor("bproj", [1, C], F32, kind="ExternalInput").ap()
    out_hbm = nc.dram_tensor("out", [TOK, C], F32, kind="ExternalOutput").ap()

    q_scr = nc.dram_tensor("q_scr", [NT, 128, 4096], BF16)
    cc_in = nc.dram_tensor("cc_in", [128, 8 * 129], F32)
    cc_out = nc.dram_tensor("cc_out", [128, 8 * 129], F32)

    ident = nc.inline_tensor(np.eye(128, dtype=ml_dtypes.bfloat16), name="ident")

    if dbg:
        d_kv = nc.dram_tensor("d_kv", [128, 8 * 129], F32, kind="ExternalOutput").ap()
        d_den = nc.dram_tensor("d_den", [128, 64], F32, kind="ExternalOutput").ap()
        d_z = nc.dram_tensor("d_z", [128, 64], F32, kind="ExternalOutput").ap()

    sb = nc.alloc_sbuf_tensor
    wqkvT_sb = sb("wqkvT_sb", [128, 8 * 3072], BF16)
    wprojT_sb = sb("wprojT_sb", [128, 8 * 1024], BF16)
    wsignT_sb = sb("wsignT_sb", [128, 8 * 1024], BF16)
    bproj_bf = sb("bproj_bf", [128, C], BF16)
    e0_row = sb("e0_row", [128, 128], BF16)
    ident_sb = sb("ident_sb", [128, 128], BF16)
    epsv = sb("epsv", [128, 1], F32)
    epsz = sb("epsz", [128, 1], F32)

    xsl = sb("xsl", [128, 4 * 1024], F32)
    ln_sites = {}
    for nm in ("L1", "L2", "L3", "L4"):
        ln_sites[nm] = {
            "stats": sb("stats" + nm, [128, 12], F32),
            "mvb": sb("mvb" + nm, [128, 2], F32),
            "rstd": sb("rstd" + nm, [128, 1], F32),
            "bvec": sb("bvec" + nm, [128, 1], F32),
            "negm": sb("negm" + nm, [128, 1], F32),
            "prev_apply": None,
        }

    ln1_tm = sb("ln1_tm", [128, 4096], BF16)
    ln1_fm = sb("ln1_fm", [128, 4096], BF16)
    q_fm = sb("q_fm", [128, 4096], BF16)
    k_tm = sb("k_tm", [128, 4096], BF16)
    v_tm = sb("v_tm", [128, 4 * 1032], BF16)
    kv_acc = sb("kv_acc", [128, 8 * 129], F32)

    kfree = sb("kfree", [128, 1024], F32)
    tmpA = sb("tmpA", [128, 1024], F32)
    tmpB = sb("tmpB", [128, 1024], F32)
    tmpC = sb("tmpC", [128, 1024], F32)
    tmpD = sb("tmpD", [128, 1024], F32)

    q2a = ln1_tm
    q2b = sb("q2b", [128, 4096], BF16)
    kv_sb = sb("kv_sb", [128, 1024], BF16)
    den_rhs = sb("den_rhs", [128, 128], BF16)
    den_sb = sb("den_sb", [128, 64], F32)
    z_sb = sb("z_sb", [128, 64], F32)
    attnz_tm = q_fm
    attnz_fm = ln1_fm
    x2 = sb("x2", [128, 3 * 1024], F32)
    ln3_tm = k_tm
    ln3_fm = v_tm
    outb = sb("outb", [128, 2 * 1024], F32)
    wstage = sb("wstage", [128, 2 * 1024], F32)

    ps_bf_a = nc.alloc_psum_tensor("ps_bf_a", [128, 1024], BF16)
    ps_bf_b = nc.alloc_psum_tensor("ps_bf_b", [128, 1024], BF16)
    ps_f_a = nc.alloc_psum_tensor("ps_f_a", [128, 512], F32)
    ps_f_b = nc.alloc_psum_tensor("ps_f_b", [128, 512], F32)
    ps_big_a = nc.alloc_psum_tensor("ps_big_a", [128, 1024], F32)
    ps_big_b = nc.alloc_psum_tensor("ps_big_b", [128, 1024], F32)

    p = Prog(nc)

    startup_x = []
    p.op("vector", lambda e: e.memset(epsv[:, :], LN_EPS), inc="dv")
    p.op("vector", lambda e: e.memset(epsz[:, :], ATTN_EPS), inc="dv")
    p.op("vector", lambda e: e.memset(kv_acc[:, :], 0.0), inc="dv")
    p.op("vector", lambda e: e.memset(e0_row[:, :], 0.0), inc="dv")
    p.op("vector", lambda e: e.memset(bproj_bf[:, :], 0.0), inc="dv")
    ev_ones = p.op("vector", lambda e: e.memset(e0_row[0:1, :], 1.0),
                   waits=[("dv", p.val("dv"))], inc="dv")
    ev_vones = p.op("vector", lambda e: e.memset(
        bass.AP(v_tm, 128, [[4 * 1032, 128], [1032, 4], [129, 8], [1, 1]]), 1.0),
        inc="dv")
    ev_consts = ("dv", p.val("dv"))
    ev_ident = p.op("sync", lambda e: e.dma_start(out=ident_sb[:], in_=ident.ap()),
                    inc="idl", inc_n=16)
    ev_wqkv_kc = []
    for kc in range(8):
        for half in range(2):
            ev = p.op("gpsimd", lambda e, kc=kc, half=half: e.dma_start(
                out=bass.AP(wqkvT_sb, kc * 3072 + half * 1536,
                            [[24576, 128], [1, 1536]]),
                in_=bass.AP(wqkvT.tensor, kc * 128 * 3072 + half * 1536,
                            [[3072, 128], [1, 1536]])),
                inc="wq%d" % kc, inc_n=16)
        ev_wqkv_kc.append(("wq%d" % kc, 32))
    ev_wqkv = None



    def layer_norm_chunk(site, src3, dst_ap, waits, apply_waits=()):
        sx = ln_sites[site]
        stats, mvb, rstd = sx["stats"], sx["mvb"], sx["rstd"]
        bvec, negm = sx["bvec"], sx["negm"]
        e1 = p.op("vector", lambda e: e.bn_stats(stats[:, 0:6], src3[0]),
                  waits=list(waits), inc="dv")
        e2 = p.op("vector", lambda e: e.bn_stats(stats[:, 6:12], src3[1]),
                  waits=list(waits), inc="dv")
        e3 = p.op("vector", lambda e: e.bn_aggr(mvb[:, :], stats[:, :]),
                  waits=[e1, e2, sx["prev_apply"]], inc="dv")
        e3b = p.op("vector", lambda e: e.tensor_scalar(
            negm[:, :], mvb[:, 0:1], -1.0, None, op0=ALU.mult),
            waits=[e3], inc="dv")
        e4 = p.op("scalar", lambda e: e.activation(
            out=rstd[:, :], in_=mvb[:, 1:2], func=AF.Ln, bias=epsv[:, :]),
            waits=[e3b], inc="ac")
        e5 = p.op("scalar", lambda e: e.activation(
            out=rstd[:, :], in_=rstd[:, :], func=AF.Exp, scale=-0.5),
            waits=[e4], inc="ac")
        e6 = p.op("scalar", lambda e: e.activation(
            out=bvec[:, :], in_=negm[:, :], func=AF.Identity,
            scale=rstd[:, :]), waits=[e5], inc="ac")
        e7 = p.op("scalar", lambda e: e.activation(
            out=dst_ap, in_=src3[2], func=AF.Identity,
            scale=rstd[:, :], bias=bvec[:, :]),
            waits=[e6] + list(apply_waits), inc="ac")
        sx["prev_apply"] = e7
        return e7

    def layer_norm_apply2(site, srcs, dsts, waits, apply_waits=()):
        sx = ln_sites[site]
        stats, mvb, rstd = sx["stats"], sx["mvb"], sx["rstd"]
        bvec, negm = sx["bvec"], sx["negm"]
        e1 = p.op("vector", lambda e: e.bn_stats(stats[:, 0:6], srcs[0]),
                  waits=list(waits), inc="dv")
        e2 = p.op("vector", lambda e: e.bn_stats(stats[:, 6:12], srcs[1]),
                  waits=list(waits), inc="dv")
        e3 = p.op("vector", lambda e: e.bn_aggr(mvb[:, :], stats[:, :]),
                  waits=[e1, e2, sx["prev_apply"]], inc="dv")
        e3b = p.op("vector", lambda e: e.tensor_scalar(
            negm[:, :], mvb[:, 0:1], -1.0, None, op0=ALU.mult),
            waits=[e3], inc="dv")
        e4 = p.op("scalar", lambda e: e.activation(
            out=rstd[:, :], in_=mvb[:, 1:2], func=AF.Ln, bias=epsv[:, :]),
            waits=[e3b], inc="ac")
        e5 = p.op("scalar", lambda e: e.activation(
            out=rstd[:, :], in_=rstd[:, :], func=AF.Exp, scale=-0.5),
            waits=[e4], inc="ac")
        e6 = p.op("scalar", lambda e: e.activation(
            out=bvec[:, :], in_=negm[:, :], func=AF.Identity,
            scale=rstd[:, :]), waits=[e5], inc="ac")
        p.op("scalar", lambda e: e.activation(
            out=dsts[0], in_=srcs[0], func=AF.Identity,
            scale=rstd[:, :], bias=bvec[:, :]),
            waits=[e6] + list(apply_waits), inc="ac")
        e8 = p.op("scalar", lambda e: e.activation(
            out=dsts[1], in_=srcs[1], func=AF.Identity,
            scale=rstd[:, :], bias=bvec[:, :]), inc="ac")
        sx["prev_apply"] = e8
        return e8

    def transpose_block(src, src_off, psum, dst, dst_off, waits):
        pe_ev = None
        for kc in range(8):
            w = waits if kc == 0 else []
            pe_ev = p.op("tensor", lambda e, kc=kc: e.transpose(
                psum[:, kc * 128:(kc + 1) * 128],
                src[:, src_off + kc * 128:src_off + (kc + 1) * 128],
                ident_sb[:, :]), waits=w, inc="pe" if kc == 7 else None)
        rowlen = dst.shape[1]
        ev = p.op("vector", lambda e: e.tensor_copy(
            bass.AP(dst, dst_off, [[rowlen, 128], [512, 8], [1, 128]]),
            psum[:, :]), waits=[pe_ev, ("dv", p.val("dv"))], inc="dv")
        return pe_ev, ev

    def elu_block(psum_ap, tmp_r, tmp_e, dst_ap, waits, dst_waits=()):
        er = p.op("scalar", lambda e: e.activation(
            out=tmp_r, in_=psum_ap, func=AF.Relu, scale=-1.0),
            waits=list(waits) + [("ac", p.val("ac")), ("dv", p.val("dv"))], inc="ac")
        ee = p.op("scalar", lambda e: e.activation(
            out=tmp_e, in_=tmp_r, func=AF.Exp, scale=-1.0), waits=[er], inc="ac")
        es = p.op("vector", lambda e: e.scalar_tensor_tensor(
            dst_ap, psum_ap, 1.0, tmp_e, op0=ALU.add, op1=ALU.max),
            waits=[ee, ("dv", p.val("dv"))] + list(dst_waits), inc="dv")
        return es

    ev_xld = {}
    xsl_free = {}
    ev_qscr = {}
    kv_evac = None
    p1bf = {0: None, 1: None}
    ln1_apply = {}
    t1_done = {}
    prev_p1 = {}
    prev_kv_ev = {}
    deferred_q = {}

    def emit_x_loads(t):
        for jj in range(NJ):
            g = t * NJ + jj
            slot = g % 4
            w = [xsl_free.get(slot)]
            ev_xld[g] = p.op("sync", lambda e, g=g, slot=slot: e.dma_start(
                out=xsl[:, slot * 1024:(slot + 1) * 1024],
                in_=x_hbm[g * 128:(g + 1) * 128, :]),
                waits=w, inc="xld%d" % slot, inc_n=16)

    def emit_ln1(t):
        evs = []
        for jj in range(NJ):
            g = t * NJ + jj
            slot = g % 4
            xa = xsl[:, slot * 1024:(slot + 1) * 1024]
            aw = [t1_done[t - 1]] if (t > 0 and jj == 0) else []
            e_apply = layer_norm_chunk(
                "L1", (xa[:, 0:512], xa[:, 512:1024], xa),
                ln1_tm[:, jj * 1024:(jj + 1) * 1024],
                [ev_xld[g]], aw)
            evs.append(e_apply)
            xsl_free[slot] = e_apply
        ln1_apply[t] = evs

    def kvacc_pair(tp, pp):
        holder = prev_p1[tp]
        ktm_evs = holder["ktm"]
        vtm_evs = holder["vtm"]
        for pr in (2 * pp, 2 * pp + 1):
            psd = ps_f_a if pr % 2 == 0 else ps_f_b
            if pr >= 2:
                bw = [prev_kv_ev[pr - 2]]
            else:
                tq = tp
                while tq >= 0 and not prev_p1[tq]["qelu"]:
                    tq -= 1
                bw = ([prev_p1[tq]["qelu"][6 + (pr % 2)]] if tq >= 0 else [])
            pe_ev = None
            for jj in range(NJ):
                w = (ktm_evs + vtm_evs + bw) if jj == 0 else []
                pe_ev = p.op("tensor", lambda e, pr=pr, jj=jj, psd=psd: e.matmul(
                    psd[:, 0:129],
                    k_tm[:, jj * 1024 + pr * 128: jj * 1024 + (pr + 1) * 128],
                    v_tm[:, jj * 1032 + pr * 129: jj * 1032 + (pr + 1) * 129],
                    start=(jj == 0), stop=(jj == NJ - 1)),
                    waits=w, inc="pe" if jj == NJ - 1 else None)
            ekv = p.op("vector", lambda e, pr=pr, psd=psd: e.tensor_tensor(
                kv_acc[:, pr * 129:(pr + 1) * 129],
                kv_acc[:, pr * 129:(pr + 1) * 129],
                psd[:, 0:129], op=ALU.add),
                waits=[pe_ev, ev_consts], inc="dv")
            prev_kv_ev[pr] = ekv

    def kvacc_group(tp):
        for pp in range(4):
            kvacc_pair(tp, pp)

    emit_x_loads(0)
    emit_ln1(0)


    emit_x_loads(0)
    emit_ln1(0)

    for t in range(NT):
        tr_evs = []
        last_tr = None
        for jj in range(NJ):
            psum = ps_bf_a if jj % 2 == 0 else ps_bf_b
            bw = [p1bf[jj % 2]]
            if t == 0 and jj == 0:
                bw = bw + [ev_ident]
            pe_ev, ev = transpose_block(
                ln1_tm, jj * 1024, psum, ln1_fm, jj * 128,
                [ln1_apply[t][jj]] + bw)
            p1bf[jj % 2] = ev
            tr_evs.append(ev)
            last_tr = pe_ev
        t1_done[t] = last_tr
        ln1fm_ready = tr_evs

        if t > 0:
            kvacc_group(t - 1)

        if t + 1 < NT:
            emit_x_loads(t + 1)
            emit_ln1(t + 1)

        q_elu_evs = {}
        ktm_evs = []
        vtm_evs = []
        kfree_evs = []

        def q_group(m):
            psq = ps_f_a if m % 2 == 0 else ps_f_b
            if m >= 2:
                bw = [q_elu_evs[m - 2]]
            else:
                bw = [prev_kv_ev.get(6 + m)]
            w0 = ln1fm_ready + bw
            pe_ev = None
            for kc in range(8):
                w = (w0 + [ev_wqkv_kc[kc]]) if t == 0 else (w0 if kc == 0 else [])
                if t == 0 and kc > 0:
                    w = [ev_wqkv_kc[kc]]
                pe_ev = p.op("tensor", lambda e, kc=kc, m=m, psq=psq: e.matmul(
                    psq[:, :],
                    wqkvT_sb[:, kc * 3072 + m * 128: kc * 3072 + (m + 1) * 128],
                    ln1_fm[:, kc * 512:(kc + 1) * 512],
                    start=(kc == 0), stop=(kc == 7)),
                    waits=w, inc="pe" if kc == 7 else None)
            tr_ = (tmpA if m % 2 == 0 else tmpB)[:, 0:512]
            te_ = (tmpC if m % 2 == 0 else tmpD)[:, 0:512]
            dw = [ev_qscr[t - 1]] if (m == 0 and t > 0) else []
            q_elu_evs[m] = elu_block(psq[:, :], tr_, te_,
                                     q_fm[:, m * 512:(m + 1) * 512], [pe_ev], dw)

        def kv_group(jj):
            if jj > 0:
                bw = [kfree_evs[jj - 1], vtm_evs[jj - 1]]
            elif t > 0:
                bw = [prev_p1[t - 1]["kfree"][3], prev_p1[t - 1]["vtm"][3]]
            else:
                bw = []
            pe_ev = None
            for kc in range(8):
                for gidx in range(4):
                    psd = ps_big_a if gidx < 2 else ps_big_b
                    col = (gidx % 2) * 512
                    w = ([tr_evs[jj]] + bw) if (kc == 0 and gidx == 0) else []
                    if t == 0 and gidx == 0:
                        w = w + [ev_wqkv_kc[kc]]
                    pe_ev = p.op("tensor", lambda e, kc=kc, jj=jj, gidx=gidx, psd=psd, col=col: e.matmul(
                        psd[:, col:col + 512],
                        ln1_fm[:, kc * 512 + jj * 128: kc * 512 + (jj + 1) * 128],
                        wqkvT_sb[:, kc * 3072 + 1024 + gidx * 512: kc * 3072 + 1024 + (gidx + 1) * 512],
                        start=(kc == 0), stop=(kc == 7)),
                        waits=w, inc="pe" if (kc == 7 and gidx == 3) else None)
            ek = elu_block(ps_big_a[:, :], tmpA[:, :], tmpC[:, :],
                           k_tm[:, jj * 1024:(jj + 1) * 1024], [pe_ev])
            ktm_evs.append(ek)
            kfree_evs.append(ek)
            evv = p.op("scalar", lambda e, jj=jj: e.copy(
                bass.AP(v_tm, jj * 1032, [[4 * 1032, 128], [129, 8], [1, 128]]),
                ps_big_b[:, :]), waits=[pe_ev, ev_vones, ("ac", p.val("ac"))], inc="ac")
            vtm_evs.append(evv)

        if t < NT - 1:
            for jj in range(NJ):
                kv_group(jj)
                q_group(2 * jj)
                q_group(2 * jj + 1)
            ev_qscr[t] = p.op("gpsimd", lambda e, t=t: e.dma_start(
                out=q_scr[t], in_=q_fm[:, :]),
                waits=[q_elu_evs[7]], inc="qs", inc_n=16)
        else:
            for jj in range(NJ):
                kv_group(jj)
            deferred_q[t] = q_group

        prev_p1[t] = {"ktm": ktm_evs, "vtm": vtm_evs, "qelu": q_elu_evs,
                      "kfree": kfree_evs}

        if t == 0:
            ev_wproj = p.op("gpsimd", lambda e: e.dma_start(
                out=wprojT_sb.ap().rearrange("p (kc o) -> p kc o", o=1024),
                in_=wprojT.rearrange("(kc p) o -> p kc o", p=128)), inc="wpj", inc_n=16)
            ev_bproj = p.op("gpsimd", lambda e: e.dma_start(
                out=bproj_bf[0:1, :], in_=bproj), waits=[ev_ones], inc="bpj", inc_n=16)
            sign_done = {}
            for kc in range(8):
                par = kc % 2
                w = [sign_done[kc - 2]] if kc >= 2 else []
                ev_wst = p.op("sync", lambda e, kc=kc, par=par: e.dma_start(
                    out=wstage[:, par * 1024:(par + 1) * 1024],
                    in_=wffnT[kc * 128:(kc + 1) * 128, :]), waits=w,
                    inc="wst%d" % par, inc_n=16)
                sign_done[kc] = p.op("scalar", lambda e, kc=kc, par=par: e.activation(
                    out=wsignT_sb[:, kc * 1024:(kc + 1) * 1024],
                    in_=wstage[:, par * 1024:(par + 1) * 1024], func=AF.Sign),
                    waits=[ev_wst, ("ac", p.val("ac"))], inc="ac")
            ev_sign = sign_done[7]


    kvacc_group(NT - 1)
    kv_evac = prev_kv_ev[7]
    kv_all_done = [prev_kv_ev[pr] for pr in range(8)]

    e_cc1 = p.op("gpsimd", lambda e: e.dma_start(out=cc_in.ap(), in_=kv_acc[:, :]),
                 waits=[kv_evac], inc="kvio", inc_n=16)
    e_cc2 = p.op("gpsimd", lambda e: e.collective_compute(
        "AllReduce", ALU.add,
        replica_groups=[[0, 1, 2, 3], [4, 5, 6, 7]],
        ins=[cc_in.ap().opt()], outs=[cc_out.ap().opt()]),
        waits=[e_cc1], inc="cc", inc_n=1)
    e_cc3 = p.op("gpsimd", lambda e: e.dma_start(out=kv_acc[:, :], in_=cc_out.ap()),
                 waits=[e_cc2], inc="kvio", inc_n=16)

    qg = deferred_q[NT - 1]
    for m in range(8):
        qg(m)
    lastq = prev_p1[NT - 1]["qelu"]
    ev_qscr[NT - 1] = p.op("gpsimd", lambda e: e.dma_start(
        out=q_scr[NT - 1], in_=q_fm[:, :]),
        waits=[lastq[7]], inc="qs", inc_n=16)

    e_m1 = p.op("vector", lambda e: e.memset(kv_sb[:, :], 0.0),
                waits=[("dv", p.val("dv"))], inc="dv")
    e_m2 = p.op("vector", lambda e: e.memset(den_rhs[:, :], 0.0), inc="dv")
    e_c1 = p.op("vector", lambda e: e.tensor_copy(
        bass.AP(kv_sb, 0, [[1024, 64], [128, 8], [1, 64]]),
        bass.AP(kv_acc, 0, [[8 * 129, 64], [129, 8], [1, 64]])),
        waits=[e_cc3, e_m1], inc="dv")
    e_c2 = p.op("vector", lambda e: e.tensor_copy(
        bass.AP(kv_sb, 64 * 1024 + 64, [[1024, 64], [128, 8], [1, 64]]),
        bass.AP(kv_acc, 64 * 8 * 129 + 64, [[8 * 129, 64], [129, 8], [1, 64]])),
        waits=[e_cc3, e_m1], inc="dv")
    e_c3 = p.op("vector", lambda e: e.tensor_copy(
        bass.AP(den_rhs, 0, [[128, 64], [18, 8], [1, 1]]),
        bass.AP(kv_acc, 128, [[8 * 129, 64], [129, 8], [1, 1]])),
        waits=[e_cc3, e_m2], inc="dv")
    e_c4 = p.op("vector", lambda e: e.tensor_copy(
        bass.AP(den_rhs, 64 * 128 + 1, [[128, 64], [18, 8], [1, 1]]),
        bass.AP(kv_acc, 64 * 8 * 129 + 128, [[8 * 129, 64], [129, 8], [1, 1]])),
        waits=[e_cc3, e_m2], inc="dv")
    kvsb_ready = [e_c1, e_c2]
    denrhs_ready = [e_c3, e_c4]

    if dbg:
        p.op("gpsimd", lambda e: e.dma_start(out=d_kv, in_=kv_acc[:, :]),
             waits=[e_cc3], inc="dbg", inc_n=16)

    ev_ql = {}
    ev_x2ld = {}
    q2buf = lambda t: q2a if t % 2 == 0 else q2b
    q2_last_use = {}

    st = {}

    def s1a_den(s):
        t, jj = divmod(s, NJ)
        q2 = q2buf(t)
        bw = []
        if s >= 1:
            bw.append(st[s - 1]["attnz"])
        pe_ev = None
        for fc in range(8):
            w = (denrhs_ready + [ev_ql[t]] + bw) if fc == 0 else []
            pe_ev = p.op("tensor", lambda e, fc=fc, jj=jj, q2=q2: e.matmul(
                ps_big_a[:, 512:528],
                q2[:, fc * 512 + jj * 128: fc * 512 + (jj + 1) * 128],
                den_rhs[:, fc * 16:(fc + 1) * 16],
                start=(fc == 0), stop=(fc == 7)),
                waits=w, inc="pe" if fc == 7 else None)
        dsw = [st[s - 4]["z"]] if s >= 4 else []
        e_den = p.op("vector", lambda e, jj=jj: e.tensor_scalar(
            den_sb[:, jj * 16:(jj + 1) * 16], ps_big_a[:, 512:528],
            epsz[:, :], None, op0=ALU.add),
            waits=[pe_ev] + dsw, inc="dv")
        e_z = p.op("vector", lambda e, jj=jj: e.reciprocal(
            z_sb[:, jj * 16:(jj + 1) * 16], den_sb[:, jj * 16:(jj + 1) * 16]),
            waits=[e_den], inc="dv")
        st[s] = {"z": e_z, "den_evac": e_den}

    def s1b_u(s):
        t, jj = divmod(s, NJ)
        q2 = q2buf(t)
        pe_ev = None
        for pr in range(8):
            if pr == 0:
                w = kvsb_ready
            elif pr == 4:
                w = [st[s]["den_evac"]]
            else:
                w = []
            pe_ev = p.op("tensor", lambda e, pr=pr, jj=jj, q2=q2: e.matmul(
                ps_big_a[:, pr * 128:(pr + 1) * 128],
                q2[:, pr * 512 + jj * 128: pr * 512 + (jj + 1) * 128],
                kv_sb[:, pr * 128:(pr + 1) * 128],
                start=True, stop=True, skip_group_check=True),
                waits=w, inc="pe" if pr == 7 else None)
        q2_last_use[t] = pe_ev
        aw = [ev_qscr[NT - 1]] if s == 0 else []
        e_attnz = p.op("vector", lambda e, jj=jj: e.tensor_tensor(
            attnz_tm[:, jj * 1024:(jj + 1) * 1024],
            ps_big_a[:, :],
            bass.AP(z_sb, jj * 16, [[64, 128], [1, 16], [0, 64]]),
            op=ALU.mult),
            waits=[pe_ev, st[s]["z"]] + aw, inc="dv")
        st[s]["attnz"] = e_attnz

    def s2aT(s):
        t, jj = divmod(s, NJ)
        bw = [st[s - 1]["afm"]] if s >= 1 else []
        pe_tr, e_afm = transpose_block(attnz_tm, jj * 1024, ps_bf_a,
                                       attnz_fm, jj * 128,
                                       [st[s]["attnz"]] + bw)
        st[s]["afm"] = e_afm

    def s2aP(s):
        t, jj = divmod(s, NJ)
        slot = s % 4
        bw = [st[s - 1]["ln2"]] if s >= 1 else []
        pe_ev = None
        for kc in range(8):
            for gidx in range(2):
                w = ([st[s]["afm"], ev_wproj, ev_ones] + bw) if (kc == 0 and gidx == 0) else []
                pe_ev = p.op("tensor", lambda e, kc=kc, jj=jj, gidx=gidx: e.matmul(
                    ps_big_b[:, gidx * 512:(gidx + 1) * 512],
                    attnz_fm[:, kc * 512 + jj * 128: kc * 512 + (jj + 1) * 128],
                    wprojT_sb[:, kc * 1024 + gidx * 512: kc * 1024 + (gidx + 1) * 512],
                    start=(kc == 0), stop=False),
                    waits=w, inc=None)
        for gidx in range(2):
            pe_ev = p.op("tensor", lambda e, gidx=gidx: e.matmul(
                ps_big_b[:, gidx * 512:(gidx + 1) * 512],
                e0_row[0:1, :],
                bproj_bf[0:1, gidx * 512:(gidx + 1) * 512],
                start=False, stop=True),
                waits=[ev_bproj] if gidx == 0 else [],
                inc="pe" if gidx == 1 else None)
        ltmp = tmpA if s % 2 == 0 else tmpC
        aw = [st[s - 2]["x2"]] if s >= 2 else []
        e_ln2 = layer_norm_chunk(
            "L2", (ps_big_b[:, 0:512], ps_big_b[:, 512:1024], ps_big_b[:, :]),
            ltmp[:, :], [pe_ev], aw)
        xp = s % 3
        bw = [st[s - 3]["out_add"]] if s >= 3 else []
        e_x2 = p.op("gpsimd", lambda e, xp=xp, slot=slot, ltmp=ltmp: e.tensor_tensor(
            x2[:, xp * 1024:(xp + 1) * 1024],
            xsl[:, slot * 1024:(slot + 1) * 1024], ltmp[:, :], op=ALU.add),
            waits=[e_ln2, ev_x2ld[s], ("gp", p.val("gp"))] + bw, inc="gp")
        xsl_free[slot] = e_x2
        if s + 4 < NS:
            emit_x_load_p2(s + 4, [e_x2])
        st[s].update({"ln2": e_ln2, "x2": e_x2})

    def s2b(s):
        t, jj = divmod(s, NJ)
        xp = s % 3
        x2c = x2[:, xp * 1024:(xp + 1) * 1024]
        aw = [st[s - 4]["tln3_pe"]] if s >= 4 else []
        e_ln3 = layer_norm_chunk(
            "L3", (x2c[:, 0:512], x2c[:, 512:1024], x2c),
            ln3_tm[:, jj * 1024:(jj + 1) * 1024], [st[s]["x2"]], aw)
        st[s]["ln3"] = e_ln3

    def s3T(s):
        t, jj = divmod(s, NJ)
        bw = [st[s - 1]["l3fm"]] if s >= 1 else []
        pe_tr, e_l3fm = transpose_block(ln3_tm, jj * 1024, ps_bf_b,
                                        ln3_fm, jj * 128,
                                        [st[s]["ln3"]] + bw)
        st[s].update({"tln3_pe": pe_tr, "l3fm": e_l3fm})

    def s3F(s):
        t, jj = divmod(s, NJ)
        bw = [st[s - 1]["ln4"]] if s >= 1 else []
        pe_ev = None
        for kc in range(8):
            for gidx in range(2):
                psd = ps_f_a if gidx == 0 else ps_f_b
                w = ([st[s]["l3fm"], ev_sign] + bw) if (kc == 0 and gidx == 0) else []
                pe_ev = p.op("tensor", lambda e, kc=kc, jj=jj, gidx=gidx, psd=psd: e.matmul(
                    psd[:, :],
                    ln3_fm[:, kc * 512 + jj * 128: kc * 512 + (jj + 1) * 128],
                    wsignT_sb[:, kc * 1024 + gidx * 512: kc * 1024 + (gidx + 1) * 512],
                    start=(kc == 0), stop=(kc == 7)),
                    waits=w, inc="pe" if (kc == 7 and gidx == 1) else None)
        ltmp = tmpB if s % 2 == 0 else tmpD
        aw = [st[s - 2]["out_add"]] if s >= 2 else []
        e_ln4 = layer_norm_apply2(
            "L4", (ps_f_a[:, :], ps_f_b[:, :]),
            (ltmp[:, 0:512], ltmp[:, 512:1024]), [pe_ev], aw)
        xp = s % 3
        oslot = s % 2
        bw = [st[s - 2]["ost"]] if s >= 2 else []
        e_out = p.op("gpsimd", lambda e, xp=xp, oslot=oslot, ltmp=ltmp: e.tensor_tensor(
            outb[:, oslot * 1024:(oslot + 1) * 1024],
            x2[:, xp * 1024:(xp + 1) * 1024], ltmp[:, :], op=ALU.add),
            waits=[e_ln4, ("gp", p.val("gp"))] + bw, inc="gp")
        e_ost = p.op("gpsimd", lambda e, s=s, oslot=oslot: e.dma_start(
            out=out_hbm[s * 128:(s + 1) * 128, :],
            in_=outb[:, oslot * 1024:(oslot + 1) * 1024]),
            waits=[e_out], inc="os%d" % oslot, inc_n=16)
        st[s].update({"ln4": e_ln4, "out_add": e_out, "ost": e_ost})

    def emit_q2_load(t):
        wq = [ev_qscr[t]]
        if t == 0:
            wq.append(kv_evac)
        elif t >= 2:
            wq.append(q2_last_use[t - 2])
        ev_ql[t] = p.op("sync", lambda e, t=t: e.dma_start(
            out=q2buf(t)[:, :], in_=q_scr[t]),
            waits=wq, inc="ql%d" % (t % 2), inc_n=16)

    def emit_x_load_p2(g, waits):
        slot = g % 4
        ev_x2ld[g] = p.op("sync", lambda e, g=g, slot=slot: e.dma_start(
            out=xsl[:, slot * 1024:(slot + 1) * 1024],
            in_=x_hbm[g * 128:(g + 1) * 128, :]),
            waits=waits, inc="xld%d" % slot, inc_n=16)

    emit_q2_load(0)
    emit_q2_load(1)
    for g in range(NJ):
        emit_x_load_p2(g, [xsl_free.get(g % 4)])
    for u in range(NS + 4):
        if 4 <= u <= NS + 3:
            s3F(u - 4)
        if 1 <= u <= NS:
            s2aT(u - 1)
        if u < NS:
            t, jj = divmod(u, NJ)
            s1a_den(u)
        if 3 <= u <= NS + 2:
            s3T(u - 3)
        if 2 <= u <= NS + 1:
            s2b(u - 2)
        if u < NS:
            s1b_u(u)
            if jj == NJ - 1 and t + 2 < NT:
                emit_q2_load(t + 2)
        if 1 <= u <= NS:
            s2aP(u - 1)

    if dbg:
        p.op("gpsimd", lambda e: e.dma_start(out=d_den, in_=den_sb[:, :]),
             waits=[("ac", p.val("ac"))], inc="dbg", inc_n=16)
        p.op("gpsimd", lambda e: e.dma_start(out=d_z, in_=z_sb[:, :]),
             waits=[("dv", p.val("dv"))], inc="dbg", inc_n=16)

    p.op("gpsimd", lambda e: None,
         waits=[("os0", p.val("os0")), ("os1", p.val("os1"))])

    with nc.Block() as block:
        p.emit(block)
    return nc


_NC_CACHE = None


def kernel(L0_attn_x, L0_attn_Wqkv, L0_attn_Wproj, L0_attn_bproj, L0_ffn_weight):
    global _NC_CACHE
    x = np.asarray(L0_attn_x, dtype=np.float32)
    wqkv = np.asarray(L0_attn_Wqkv, dtype=np.float32)
    wproj = np.asarray(L0_attn_Wproj, dtype=np.float32)
    bproj = np.asarray(L0_attn_bproj, dtype=np.float32).reshape(1, C)
    wffn = np.asarray(L0_ffn_weight, dtype=np.float32)

    wqkvT = np.ascontiguousarray(wqkv.T)
    wprojT = np.ascontiguousarray(wproj.T)
    wffnT = np.ascontiguousarray(wffn.T)

    if _NC_CACHE is None:
        _NC_CACHE = build_program()
    nc = _NC_CACHE

    in_maps = []
    for c in range(8):
        b = c // 4
        r0 = (c % 4) * TOK
        in_maps.append({
            "x": np.ascontiguousarray(x[b, r0:r0 + TOK, :]),
            "wqkvT": wqkvT,
            "wprojT": wprojT,
            "wffnT": wffnT,
            "bproj": bproj,
        })
    res = run_bass_kernel_spmd(nc, in_maps, core_ids=list(range(8)))
    out = np.empty((2, 16384, C), dtype=np.float32)
    for c in range(8):
        b = c // 4
        r0 = (c % 4) * TOK
        out[b, r0:r0 + TOK, :] = res.results[c]["out"]
    return out


# version 24
# speedup vs baseline: 1.0085x; 1.0085x over previous
import numpy as np
import ml_dtypes
import concourse.bass as bass
import concourse.mybir as mybir
from concourse.bass_utils import run_bass_kernel_spmd

F32 = mybir.dt.float32
BF16 = mybir.dt.bfloat16
AF = mybir.ActivationFunctionType
ALU = mybir.AluOpType

ENGINES = ("tensor", "vector", "scalar", "gpsimd", "sync")

C = 1024
TOK = 4096
TW = 512
NT = TOK // TW
NJ = TW // 128
NS = NT * NJ
LN_EPS = 1e-5
ATTN_EPS = 1e-6


class Prog:
    def __init__(self, nc):
        self.nc = nc
        self.sems = {}
        self.counts = {}
        self.stream = {e: [] for e in ENGINES}

    def sem(self, name):
        if name not in self.sems:
            self.sems[name] = self.nc.alloc_semaphore(name=name)
            self.counts[name] = 0
        return self.sems[name]

    def val(self, name):
        return self.counts.get(name, 0)

    def op(self, eng, builder, waits=(), inc=None, inc_n=1):
        wl = {}
        for w in waits:
            if w is None:
                continue
            (sname, v) = w
            if v > 0:
                self.sem(sname)
                wl[sname] = max(wl.get(sname, 0), v)
        if inc is not None:
            self.sem(inc)
            self.counts[inc] += inc_n
        self.stream[eng].append((builder, sorted(wl.items()), inc, inc_n))
        return (inc, self.counts[inc]) if inc is not None else None

    def emit(self, block):
        def mk(eng_name):
            def body(eng):
                for builder, waits, inc, inc_n in self.stream[eng_name]:
                    for (sname, v) in waits:
                        eng.wait_ge(self.sems[sname], v)
                    if builder is not None:
                        inst = builder(eng)
                        if inc is not None:
                            if inst is None:
                                inst = eng.nop()
                            inst.then_inc(self.sems[inc], inc_n)
            return body

        for e in ENGINES:
            if self.stream[e]:
                getattr(block, e)(mk(e))


def build_program(dbg=False):
    nc = bass.Bass("TRN2", target_bir_lowering=False, debug=False)

    x_hbm = nc.dram_tensor("x", [TOK, C], F32, kind="ExternalInput").ap()
    wqkvT = nc.dram_tensor("wqkvT", [C, 3 * C], F32, kind="ExternalInput").ap()
    wprojT = nc.dram_tensor("wprojT", [C, C], F32, kind="ExternalInput").ap()
    wffnT = nc.dram_tensor("wffnT", [C, C], F32, kind="ExternalInput").ap()
    bproj = nc.dram_tensor("bproj", [1, C], F32, kind="ExternalInput").ap()
    out_hbm = nc.dram_tensor("out", [TOK, C], F32, kind="ExternalOutput").ap()

    q_scr = nc.dram_tensor("q_scr", [NT, 128, 4096], BF16)
    cc_in = nc.dram_tensor("cc_in", [128, 8 * 129], F32)
    cc_out = nc.dram_tensor("cc_out", [128, 8 * 129], F32)

    ident = nc.inline_tensor(np.eye(128, dtype=ml_dtypes.bfloat16), name="ident")

    if dbg:
        d_kv = nc.dram_tensor("d_kv", [128, 8 * 129], F32, kind="ExternalOutput").ap()
        d_den = nc.dram_tensor("d_den", [128, 64], F32, kind="ExternalOutput").ap()
        d_z = nc.dram_tensor("d_z", [128, 64], F32, kind="ExternalOutput").ap()

    sb = nc.alloc_sbuf_tensor
    wqkvT_sb = sb("wqkvT_sb", [128, 8 * 3072], BF16)
    wprojT_sb = sb("wprojT_sb", [128, 8 * 1024], BF16)
    wsignT_sb = sb("wsignT_sb", [128, 8 * 1024], BF16)
    bproj_bf = sb("bproj_bf", [128, C], BF16)
    e0_row = sb("e0_row", [128, 128], BF16)
    ident_sb = sb("ident_sb", [128, 128], BF16)
    epsv = sb("epsv", [128, 1], F32)
    epsz = sb("epsz", [128, 1], F32)

    xsl = sb("xsl", [128, 4 * 1024], F32)
    ln_sites = {}
    for nm in ("L1", "L2", "L3", "L4"):
        ln_sites[nm] = {
            "stats": sb("stats" + nm, [128, 12], F32),
            "mvb": sb("mvb" + nm, [128, 2], F32),
            "rstd": sb("rstd" + nm, [128, 1], F32),
            "bvec": sb("bvec" + nm, [128, 1], F32),
            "negm": sb("negm" + nm, [128, 1], F32),
            "prev_apply": None,
        }

    ln1_tm = sb("ln1_tm", [128, 4096], BF16)
    ln1_fm = sb("ln1_fm", [128, 4096], BF16)
    q_fm = sb("q_fm", [128, 4096], BF16)
    k_tm = sb("k_tm", [128, 4096], BF16)
    v_tm = sb("v_tm", [128, 4 * 1032], BF16)
    kv_acc = sb("kv_acc", [128, 8 * 129], F32)

    kfree = sb("kfree", [128, 1024], F32)
    tmpA = sb("tmpA", [128, 1024], F32)
    tmpB = sb("tmpB", [128, 1024], F32)
    tmpC = sb("tmpC", [128, 1024], F32)
    tmpD = sb("tmpD", [128, 1024], F32)

    q2a = ln1_tm
    q2b = sb("q2b", [128, 4096], BF16)
    kv_sb = sb("kv_sb", [128, 1024], BF16)
    den_rhs = sb("den_rhs", [128, 128], BF16)
    den_sb = sb("den_sb", [128, 64], F32)
    z_sb = sb("z_sb", [128, 64], F32)
    attnz_tm = q_fm
    attnz_fm = ln1_fm
    x2 = sb("x2", [128, 3 * 1024], F32)
    ln3_tm = k_tm
    ln3_fm = v_tm
    outb = sb("outb", [128, 2 * 1024], F32)
    wstage = sb("wstage", [128, 2 * 1024], F32)

    ps_bf_a = nc.alloc_psum_tensor("ps_bf_a", [128, 1024], BF16)
    ps_bf_b = nc.alloc_psum_tensor("ps_bf_b", [128, 1024], BF16)
    ps_f_a = nc.alloc_psum_tensor("ps_f_a", [128, 512], F32)
    ps_f_b = nc.alloc_psum_tensor("ps_f_b", [128, 512], F32)
    ps_big_a = nc.alloc_psum_tensor("ps_big_a", [128, 1024], F32)
    ps_big_b = nc.alloc_psum_tensor("ps_big_b", [128, 1024], F32)

    p = Prog(nc)

    startup_x = []
    p.op("vector", lambda e: e.memset(epsv[:, :], LN_EPS), inc="dv")
    p.op("vector", lambda e: e.memset(epsz[:, :], ATTN_EPS), inc="dv")
    p.op("vector", lambda e: e.memset(kv_acc[:, :], 0.0), inc="dv")
    p.op("vector", lambda e: e.memset(e0_row[:, :], 0.0), inc="dv")
    p.op("vector", lambda e: e.memset(bproj_bf[:, :], 0.0), inc="dv")
    ev_ones = p.op("vector", lambda e: e.memset(e0_row[0:1, :], 1.0),
                   waits=[("dv", p.val("dv"))], inc="dv")
    ev_vones = p.op("vector", lambda e: e.memset(
        bass.AP(v_tm, 128, [[4 * 1032, 128], [1032, 4], [129, 8], [1, 1]]), 1.0),
        inc="dv")
    ev_consts = ("dv", p.val("dv"))
    ev_ident = p.op("sync", lambda e: e.dma_start(out=ident_sb[:], in_=ident.ap()),
                    inc="idl", inc_n=16)
    ev_wqkv_kc = []
    for kc in range(8):
        for half in range(2):
            ev = p.op("gpsimd", lambda e, kc=kc, half=half: e.dma_start(
                out=bass.AP(wqkvT_sb, kc * 3072 + half * 1536,
                            [[24576, 128], [1, 1536]]),
                in_=bass.AP(wqkvT.tensor, kc * 128 * 3072 + half * 1536,
                            [[3072, 128], [1, 1536]])),
                inc="wq%d" % kc, inc_n=16)
        ev_wqkv_kc.append(("wq%d" % kc, 32))
    ev_wqkv = None



    def layer_norm_chunk(site, src3, dst_ap, waits, apply_waits=()):
        sx = ln_sites[site]
        stats, mvb, rstd = sx["stats"], sx["mvb"], sx["rstd"]
        bvec, negm = sx["bvec"], sx["negm"]
        e1 = p.op("vector", lambda e: e.bn_stats(stats[:, 0:6], src3[0]),
                  waits=list(waits), inc="dv")
        e2 = p.op("vector", lambda e: e.bn_stats(stats[:, 6:12], src3[1]),
                  waits=list(waits), inc="dv")
        e3 = p.op("vector", lambda e: e.bn_aggr(mvb[:, :], stats[:, :]),
                  waits=[e1, e2, sx["prev_apply"]], inc="dv")
        e3b = p.op("vector", lambda e: e.tensor_scalar(
            negm[:, :], mvb[:, 0:1], -1.0, None, op0=ALU.mult),
            waits=[e3], inc="dv")
        e4 = p.op("scalar", lambda e: e.activation(
            out=rstd[:, :], in_=mvb[:, 1:2], func=AF.Ln, bias=epsv[:, :]),
            waits=[e3b], inc="ac")
        e5 = p.op("scalar", lambda e: e.activation(
            out=rstd[:, :], in_=rstd[:, :], func=AF.Exp, scale=-0.5),
            waits=[e4], inc="ac")
        e6 = p.op("scalar", lambda e: e.activation(
            out=bvec[:, :], in_=negm[:, :], func=AF.Identity,
            scale=rstd[:, :]), waits=[e5], inc="ac")
        e7 = p.op("scalar", lambda e: e.activation(
            out=dst_ap, in_=src3[2], func=AF.Identity,
            scale=rstd[:, :], bias=bvec[:, :]),
            waits=[e6] + list(apply_waits), inc="ac")
        sx["prev_apply"] = e7
        return e7

    def layer_norm_apply2(site, srcs, dsts, waits, apply_waits=()):
        sx = ln_sites[site]
        stats, mvb, rstd = sx["stats"], sx["mvb"], sx["rstd"]
        bvec, negm = sx["bvec"], sx["negm"]
        e1 = p.op("vector", lambda e: e.bn_stats(stats[:, 0:6], srcs[0]),
                  waits=list(waits), inc="dv")
        e2 = p.op("vector", lambda e: e.bn_stats(stats[:, 6:12], srcs[1]),
                  waits=list(waits), inc="dv")
        e3 = p.op("vector", lambda e: e.bn_aggr(mvb[:, :], stats[:, :]),
                  waits=[e1, e2, sx["prev_apply"]], inc="dv")
        e3b = p.op("vector", lambda e: e.tensor_scalar(
            negm[:, :], mvb[:, 0:1], -1.0, None, op0=ALU.mult),
            waits=[e3], inc="dv")
        e4 = p.op("scalar", lambda e: e.activation(
            out=rstd[:, :], in_=mvb[:, 1:2], func=AF.Ln, bias=epsv[:, :]),
            waits=[e3b], inc="ac")
        e5 = p.op("scalar", lambda e: e.activation(
            out=rstd[:, :], in_=rstd[:, :], func=AF.Exp, scale=-0.5),
            waits=[e4], inc="ac")
        e6 = p.op("scalar", lambda e: e.activation(
            out=bvec[:, :], in_=negm[:, :], func=AF.Identity,
            scale=rstd[:, :]), waits=[e5], inc="ac")
        p.op("scalar", lambda e: e.activation(
            out=dsts[0], in_=srcs[0], func=AF.Identity,
            scale=rstd[:, :], bias=bvec[:, :]),
            waits=[e6] + list(apply_waits), inc="ac")
        e8 = p.op("scalar", lambda e: e.activation(
            out=dsts[1], in_=srcs[1], func=AF.Identity,
            scale=rstd[:, :], bias=bvec[:, :]), inc="ac")
        sx["prev_apply"] = e8
        return e8

    def transpose_block(src, src_off, psum, dst, dst_off, waits):
        pe_ev = None
        for kc in range(8):
            w = waits if kc == 0 else []
            pe_ev = p.op("tensor", lambda e, kc=kc: e.transpose(
                psum[:, kc * 128:(kc + 1) * 128],
                src[:, src_off + kc * 128:src_off + (kc + 1) * 128],
                ident_sb[:, :]), waits=w, inc="pe" if kc == 7 else None)
        rowlen = dst.shape[1]
        ev = p.op("vector", lambda e: e.tensor_copy(
            bass.AP(dst, dst_off, [[rowlen, 128], [512, 8], [1, 128]]),
            psum[:, :]), waits=[pe_ev, ("dv", p.val("dv"))], inc="dv")
        return pe_ev, ev

    def elu_block(psum_ap, tmp_r, tmp_e, dst_ap, waits, dst_waits=()):
        er = p.op("scalar", lambda e: e.activation(
            out=tmp_r, in_=psum_ap, func=AF.Relu, scale=-1.0),
            waits=list(waits) + [("ac", p.val("ac")), ("dv", p.val("dv"))], inc="ac")
        ee = p.op("scalar", lambda e: e.activation(
            out=tmp_e, in_=tmp_r, func=AF.Exp, scale=-1.0), waits=[er], inc="ac")
        es = p.op("vector", lambda e: e.scalar_tensor_tensor(
            dst_ap, psum_ap, 1.0, tmp_e, op0=ALU.add, op1=ALU.max),
            waits=[ee, ("dv", p.val("dv"))] + list(dst_waits), inc="dv")
        return es

    ev_xld = {}
    xsl_free = {}
    ev_qscr = {}
    kv_evac = None
    p1bf = {0: None, 1: None}
    ln1_apply = {}
    t1_done = {}
    prev_p1 = {}
    prev_kv_ev = {}
    deferred_q = {}

    def emit_x_loads(t):
        for jj in range(NJ):
            g = t * NJ + jj
            slot = g % 4
            w = [xsl_free.get(slot)]
            ev_xld[g] = p.op("sync", lambda e, g=g, slot=slot: e.dma_start(
                out=xsl[:, slot * 1024:(slot + 1) * 1024],
                in_=x_hbm[g * 128:(g + 1) * 128, :]),
                waits=w, inc="xld%d" % slot, inc_n=16)

    def emit_ln1(t):
        evs = []
        for jj in range(NJ):
            g = t * NJ + jj
            slot = g % 4
            xa = xsl[:, slot * 1024:(slot + 1) * 1024]
            aw = [t1_done[t - 1]] if (t > 0 and jj == 0) else []
            e_apply = layer_norm_chunk(
                "L1", (xa[:, 0:512], xa[:, 512:1024], xa),
                ln1_tm[:, jj * 1024:(jj + 1) * 1024],
                [ev_xld[g]], aw)
            evs.append(e_apply)
            xsl_free[slot] = e_apply
        ln1_apply[t] = evs

    def kvacc_pair(tp, pp):
        holder = prev_p1[tp]
        ktm_evs = holder["ktm"]
        vtm_evs = holder["vtm"]
        for pr in (2 * pp, 2 * pp + 1):
            psd = ps_f_a if pr % 2 == 0 else ps_f_b
            if pr >= 2:
                bw = [prev_kv_ev[pr - 2]]
            else:
                tq = tp
                while tq >= 0 and not prev_p1[tq]["qelu"]:
                    tq -= 1
                bw = ([prev_p1[tq]["qelu"][6 + (pr % 2)]] if tq >= 0 else [])
            pe_ev = None
            for jj in range(NJ):
                w = (ktm_evs + vtm_evs + bw) if jj == 0 else []
                pe_ev = p.op("tensor", lambda e, pr=pr, jj=jj, psd=psd: e.matmul(
                    psd[:, 0:129],
                    k_tm[:, jj * 1024 + pr * 128: jj * 1024 + (pr + 1) * 128],
                    v_tm[:, jj * 1032 + pr * 129: jj * 1032 + (pr + 1) * 129],
                    start=(jj == 0), stop=(jj == NJ - 1)),
                    waits=w, inc="pe" if jj == NJ - 1 else None)
            ekv = p.op("vector", lambda e, pr=pr, psd=psd: e.tensor_tensor(
                kv_acc[:, pr * 129:(pr + 1) * 129],
                kv_acc[:, pr * 129:(pr + 1) * 129],
                psd[:, 0:129], op=ALU.add),
                waits=[pe_ev, ev_consts], inc="dv")
            prev_kv_ev[pr] = ekv

    def kvacc_group(tp):
        for pp in range(4):
            kvacc_pair(tp, pp)

    emit_x_loads(0)
    emit_ln1(0)


    emit_x_loads(0)
    emit_ln1(0)

    for t in range(NT):
        tr_evs = []
        last_tr = None
        for jj in range(NJ):
            psum = ps_bf_a if jj % 2 == 0 else ps_bf_b
            bw = [p1bf[jj % 2]]
            if t == 0 and jj == 0:
                bw = bw + [ev_ident]
            pe_ev, ev = transpose_block(
                ln1_tm, jj * 1024, psum, ln1_fm, jj * 128,
                [ln1_apply[t][jj]] + bw)
            p1bf[jj % 2] = ev
            tr_evs.append(ev)
            last_tr = pe_ev
        t1_done[t] = last_tr
        ln1fm_ready = tr_evs

        if t > 0:
            kvacc_group(t - 1)

        if t + 1 < NT:
            emit_x_loads(t + 1)
            emit_ln1(t + 1)

        q_elu_evs = {}
        ktm_evs = []
        vtm_evs = []
        kfree_evs = []

        def q_group(m):
            psq = ps_f_a if m % 2 == 0 else ps_f_b
            if m >= 2:
                bw = [q_elu_evs[m - 2]]
            else:
                bw = [prev_kv_ev.get(6 + m)]
            w0 = ln1fm_ready + bw
            pe_ev = None
            for kc in range(8):
                w = (w0 + [ev_wqkv_kc[kc]]) if t == 0 else (w0 if kc == 0 else [])
                if t == 0 and kc > 0:
                    w = [ev_wqkv_kc[kc]]
                pe_ev = p.op("tensor", lambda e, kc=kc, m=m, psq=psq: e.matmul(
                    psq[:, :],
                    wqkvT_sb[:, kc * 3072 + m * 128: kc * 3072 + (m + 1) * 128],
                    ln1_fm[:, kc * 512:(kc + 1) * 512],
                    start=(kc == 0), stop=(kc == 7)),
                    waits=w, inc="pe" if kc == 7 else None)
            tr_ = (tmpA if m % 2 == 0 else tmpB)[:, 0:512]
            te_ = (tmpC if m % 2 == 0 else tmpD)[:, 0:512]
            dw = [ev_qscr[t - 1]] if (m == 0 and t > 0) else []
            q_elu_evs[m] = elu_block(psq[:, :], tr_, te_,
                                     q_fm[:, m * 512:(m + 1) * 512], [pe_ev], dw)

        def kv_group(jj):
            if jj > 0:
                bw = [kfree_evs[jj - 1], vtm_evs[jj - 1]]
            elif t > 0:
                bw = [prev_p1[t - 1]["kfree"][3], prev_p1[t - 1]["vtm"][3]]
            else:
                bw = []
            pe_ev = None
            for kc in range(8):
                for gidx in range(4):
                    psd = ps_big_a if gidx < 2 else ps_big_b
                    col = (gidx % 2) * 512
                    w = ([tr_evs[jj]] + bw) if (kc == 0 and gidx == 0) else []
                    if t == 0 and gidx == 0:
                        w = w + [ev_wqkv_kc[kc]]
                    pe_ev = p.op("tensor", lambda e, kc=kc, jj=jj, gidx=gidx, psd=psd, col=col: e.matmul(
                        psd[:, col:col + 512],
                        ln1_fm[:, kc * 512 + jj * 128: kc * 512 + (jj + 1) * 128],
                        wqkvT_sb[:, kc * 3072 + 1024 + gidx * 512: kc * 3072 + 1024 + (gidx + 1) * 512],
                        start=(kc == 0), stop=(kc == 7)),
                        waits=w, inc="pe" if (kc == 7 and gidx == 3) else None)
            ek = elu_block(ps_big_a[:, :], tmpA[:, :], tmpC[:, :],
                           k_tm[:, jj * 1024:(jj + 1) * 1024], [pe_ev])
            ktm_evs.append(ek)
            kfree_evs.append(ek)
            evv = p.op("scalar", lambda e, jj=jj: e.copy(
                bass.AP(v_tm, jj * 1032, [[4 * 1032, 128], [129, 8], [1, 128]]),
                ps_big_b[:, :]), waits=[pe_ev, ev_vones, ("ac", p.val("ac"))], inc="ac")
            vtm_evs.append(evv)

        if t < NT - 1:
            for jj in range(NJ):
                kv_group(jj)
                q_group(2 * jj)
                q_group(2 * jj + 1)
            ev_qscr[t] = p.op("gpsimd", lambda e, t=t: e.dma_start(
                out=q_scr[t], in_=q_fm[:, :]),
                waits=[q_elu_evs[7]], inc="qs", inc_n=16)
        else:
            for jj in range(NJ):
                kv_group(jj)
            deferred_q[t] = q_group

        prev_p1[t] = {"ktm": ktm_evs, "vtm": vtm_evs, "qelu": q_elu_evs,
                      "kfree": kfree_evs}

        if t == 0:
            ev_wproj = p.op("gpsimd", lambda e: e.dma_start(
                out=wprojT_sb.ap().rearrange("p (kc o) -> p kc o", o=1024),
                in_=wprojT.rearrange("(kc p) o -> p kc o", p=128)), inc="wpj", inc_n=16)
            ev_bproj = p.op("gpsimd", lambda e: e.dma_start(
                out=bproj_bf[0:1, :], in_=bproj), waits=[ev_ones], inc="bpj", inc_n=16)
            sign_done = {}
            for kc in range(8):
                par = kc % 2
                w = [sign_done[kc - 2]] if kc >= 2 else []
                ev_wst = p.op("sync", lambda e, kc=kc, par=par: e.dma_start(
                    out=wstage[:, par * 1024:(par + 1) * 1024],
                    in_=wffnT[kc * 128:(kc + 1) * 128, :]), waits=w,
                    inc="wst%d" % par, inc_n=16)
                sign_done[kc] = p.op("scalar", lambda e, kc=kc, par=par: e.activation(
                    out=wsignT_sb[:, kc * 1024:(kc + 1) * 1024],
                    in_=wstage[:, par * 1024:(par + 1) * 1024], func=AF.Sign),
                    waits=[ev_wst, ("ac", p.val("ac"))], inc="ac")
            ev_sign = sign_done[7]


    kvacc_group(NT - 1)
    kv_evac = prev_kv_ev[7]
    kv_all_done = [prev_kv_ev[pr] for pr in range(8)]

    e_cc1 = p.op("gpsimd", lambda e: e.dma_start(out=cc_in.ap(), in_=kv_acc[:, :]),
                 waits=[kv_evac], inc="kvio", inc_n=16)
    e_cc2 = p.op("gpsimd", lambda e: e.collective_compute(
        "AllReduce", ALU.add,
        replica_groups=[[0, 1, 2, 3], [4, 5, 6, 7]],
        ins=[cc_in.ap().opt()], outs=[cc_out.ap().opt()]),
        waits=[e_cc1], inc="cc", inc_n=1)
    e_cc3 = p.op("gpsimd", lambda e: e.dma_start(out=kv_acc[:, :], in_=cc_out.ap()),
                 waits=[e_cc2], inc="kvio", inc_n=16)

    qg = deferred_q[NT - 1]
    for m in range(8):
        qg(m)
    lastq = prev_p1[NT - 1]["qelu"]
    ev_qscr[NT - 1] = p.op("gpsimd", lambda e: e.dma_start(
        out=q_scr[NT - 1], in_=q_fm[:, :]),
        waits=[lastq[7]], inc="qs", inc_n=16)

    e_m1 = p.op("vector", lambda e: e.memset(kv_sb[:, :], 0.0),
                waits=[("dv", p.val("dv"))], inc="dv")
    e_m2 = p.op("vector", lambda e: e.memset(den_rhs[:, :], 0.0), inc="dv")
    e_c1 = p.op("vector", lambda e: e.tensor_copy(
        bass.AP(kv_sb, 0, [[1024, 64], [128, 8], [1, 64]]),
        bass.AP(kv_acc, 0, [[8 * 129, 64], [129, 8], [1, 64]])),
        waits=[e_cc3, e_m1], inc="dv")
    e_c2 = p.op("vector", lambda e: e.tensor_copy(
        bass.AP(kv_sb, 64 * 1024 + 64, [[1024, 64], [128, 8], [1, 64]]),
        bass.AP(kv_acc, 64 * 8 * 129 + 64, [[8 * 129, 64], [129, 8], [1, 64]])),
        waits=[e_cc3, e_m1], inc="dv")
    e_c3 = p.op("vector", lambda e: e.tensor_copy(
        bass.AP(den_rhs, 0, [[128, 64], [18, 8], [1, 1]]),
        bass.AP(kv_acc, 128, [[8 * 129, 64], [129, 8], [1, 1]])),
        waits=[e_cc3, e_m2], inc="dv")
    e_c4 = p.op("vector", lambda e: e.tensor_copy(
        bass.AP(den_rhs, 64 * 128 + 1, [[128, 64], [18, 8], [1, 1]]),
        bass.AP(kv_acc, 64 * 8 * 129 + 128, [[8 * 129, 64], [129, 8], [1, 1]])),
        waits=[e_cc3, e_m2], inc="dv")
    kvsb_ready = [e_c1, e_c2]
    denrhs_ready = [e_c3, e_c4]

    if dbg:
        p.op("gpsimd", lambda e: e.dma_start(out=d_kv, in_=kv_acc[:, :]),
             waits=[e_cc3], inc="dbg", inc_n=16)

    ev_ql = {}
    ev_x2ld = {}
    q2buf = lambda t: q2a if t % 2 == 0 else q2b
    q2_last_use = {}

    st = {}

    def s1a_den(s):
        t, jj = divmod(s, NJ)
        q2 = q2buf(t)
        bw = []
        if s >= 1:
            bw.append(st[s - 1]["attnz"])
        pe_ev = None
        for fc in range(8):
            w = (denrhs_ready + [ev_ql[t]] + bw) if fc == 0 else []
            pe_ev = p.op("tensor", lambda e, fc=fc, jj=jj, q2=q2: e.matmul(
                ps_big_a[:, 512:528],
                q2[:, fc * 512 + jj * 128: fc * 512 + (jj + 1) * 128],
                den_rhs[:, fc * 16:(fc + 1) * 16],
                start=(fc == 0), stop=(fc == 7)),
                waits=w, inc="pe" if fc == 7 else None)
        dsw = [st[s - 4]["z"]] if s >= 4 else []
        e_den = p.op("vector", lambda e, jj=jj: e.tensor_scalar(
            den_sb[:, jj * 16:(jj + 1) * 16], ps_big_a[:, 512:528],
            epsz[:, :], None, op0=ALU.add),
            waits=[pe_ev] + dsw, inc="dv")
        e_z = p.op("vector", lambda e, jj=jj: e.reciprocal(
            z_sb[:, jj * 16:(jj + 1) * 16], den_sb[:, jj * 16:(jj + 1) * 16]),
            waits=[e_den], inc="dv")
        st[s] = {"z": e_z, "den_evac": e_den}

    def s1b_u(s):
        t, jj = divmod(s, NJ)
        q2 = q2buf(t)
        pe_ev = None
        for pr in range(8):
            if pr == 0:
                w = kvsb_ready
            elif pr == 4:
                w = [st[s]["den_evac"]]
            else:
                w = []
            pe_ev = p.op("tensor", lambda e, pr=pr, jj=jj, q2=q2: e.matmul(
                ps_big_a[:, pr * 128:(pr + 1) * 128],
                q2[:, pr * 512 + jj * 128: pr * 512 + (jj + 1) * 128],
                kv_sb[:, pr * 128:(pr + 1) * 128],
                start=True, stop=True, skip_group_check=True),
                waits=w, inc="pe" if pr == 7 else None)
        q2_last_use[t] = pe_ev
        aw = [ev_qscr[NT - 1]] if s == 0 else []
        e_attnz = p.op("vector", lambda e, jj=jj: e.tensor_tensor(
            attnz_tm[:, jj * 1024:(jj + 1) * 1024],
            ps_big_a[:, :],
            bass.AP(z_sb, jj * 16, [[64, 128], [1, 16], [0, 64]]),
            op=ALU.mult),
            waits=[pe_ev, st[s]["z"]] + aw, inc="dv")
        st[s]["attnz"] = e_attnz

    def s2aT(s):
        t, jj = divmod(s, NJ)
        bw = [st[s - 1]["afm"]] if s >= 1 else []
        pe_tr, e_afm = transpose_block(attnz_tm, jj * 1024, ps_bf_a,
                                       attnz_fm, jj * 128,
                                       [st[s]["attnz"]] + bw)
        st[s]["afm"] = e_afm

    def s2aP(s):
        t, jj = divmod(s, NJ)
        slot = s % 4
        bw = [st[s - 1]["ln2"]] if s >= 1 else []
        pe_ev = None
        for kc in range(8):
            for gidx in range(2):
                w = ([st[s]["afm"], ev_wproj, ev_ones] + bw) if (kc == 0 and gidx == 0) else []
                pe_ev = p.op("tensor", lambda e, kc=kc, jj=jj, gidx=gidx: e.matmul(
                    ps_big_b[:, gidx * 512:(gidx + 1) * 512],
                    attnz_fm[:, kc * 512 + jj * 128: kc * 512 + (jj + 1) * 128],
                    wprojT_sb[:, kc * 1024 + gidx * 512: kc * 1024 + (gidx + 1) * 512],
                    start=(kc == 0), stop=False),
                    waits=w, inc=None)
        for gidx in range(2):
            pe_ev = p.op("tensor", lambda e, gidx=gidx: e.matmul(
                ps_big_b[:, gidx * 512:(gidx + 1) * 512],
                e0_row[0:1, :],
                bproj_bf[0:1, gidx * 512:(gidx + 1) * 512],
                start=False, stop=True),
                waits=[ev_bproj] if gidx == 0 else [],
                inc="pe" if gidx == 1 else None)
        ltmp = tmpA if s % 2 == 0 else tmpC
        aw = [st[s - 2]["x2"]] if s >= 2 else []
        e_ln2 = layer_norm_chunk(
            "L2", (ps_big_b[:, 0:512], ps_big_b[:, 512:1024], ps_big_b[:, :]),
            ltmp[:, :], [pe_ev], aw)
        xp = s % 3
        bw = [st[s - 3]["out_add"]] if s >= 3 else []
        e_x2 = p.op("gpsimd", lambda e, xp=xp, slot=slot, ltmp=ltmp: e.tensor_tensor(
            x2[:, xp * 1024:(xp + 1) * 1024],
            xsl[:, slot * 1024:(slot + 1) * 1024], ltmp[:, :], op=ALU.add),
            waits=[e_ln2, ev_x2ld[s], ("gp", p.val("gp"))] + bw, inc="gp")
        xsl_free[slot] = e_x2
        if s + 4 < NS:
            emit_x_load_p2(s + 4, [e_x2])
        st[s].update({"ln2": e_ln2, "x2": e_x2})

    def s2b(s):
        t, jj = divmod(s, NJ)
        xp = s % 3
        x2c = x2[:, xp * 1024:(xp + 1) * 1024]
        aw = [st[s - 4]["tln3_pe"]] if s >= 4 else []
        e_ln3 = layer_norm_chunk(
            "L3", (x2c[:, 0:512], x2c[:, 512:1024], x2c),
            ln3_tm[:, jj * 1024:(jj + 1) * 1024], [st[s]["x2"]], aw)
        st[s]["ln3"] = e_ln3

    def s3T(s):
        t, jj = divmod(s, NJ)
        bw = [st[s - 1]["l3fm"]] if s >= 1 else []
        pe_tr, e_l3fm = transpose_block(ln3_tm, jj * 1024, ps_bf_b,
                                        ln3_fm, jj * 128,
                                        [st[s]["ln3"]] + bw)
        st[s].update({"tln3_pe": pe_tr, "l3fm": e_l3fm})

    def s3F(s):
        t, jj = divmod(s, NJ)
        bw = [st[s - 1]["ln4"]] if s >= 1 else []
        pe_ev = None
        for kc in range(8):
            for gidx in range(2):
                psd = ps_f_a if gidx == 0 else ps_f_b
                w = ([st[s]["l3fm"], ev_sign] + bw) if (kc == 0 and gidx == 0) else []
                pe_ev = p.op("tensor", lambda e, kc=kc, jj=jj, gidx=gidx, psd=psd: e.matmul(
                    psd[:, :],
                    ln3_fm[:, kc * 512 + jj * 128: kc * 512 + (jj + 1) * 128],
                    wsignT_sb[:, kc * 1024 + gidx * 512: kc * 1024 + (gidx + 1) * 512],
                    start=(kc == 0), stop=(kc == 7)),
                    waits=w, inc="pe" if (kc == 7 and gidx == 1) else None)
        ltmp = tmpB if s % 2 == 0 else tmpD
        aw = [st[s - 2]["out_add"]] if s >= 2 else []
        e_ln4 = layer_norm_apply2(
            "L4", (ps_f_a[:, :], ps_f_b[:, :]),
            (ltmp[:, 0:512], ltmp[:, 512:1024]), [pe_ev], aw)
        xp = s % 3
        oslot = s % 2
        bw = [st[s - 2]["ost"]] if s >= 2 else []
        e_out = p.op("gpsimd", lambda e, xp=xp, oslot=oslot, ltmp=ltmp: e.tensor_tensor(
            outb[:, oslot * 1024:(oslot + 1) * 1024],
            x2[:, xp * 1024:(xp + 1) * 1024], ltmp[:, :], op=ALU.add),
            waits=[e_ln4, ("gp", p.val("gp"))] + bw, inc="gp")
        e_ost = p.op("sync", lambda e, s=s, oslot=oslot: e.dma_start(
            out=out_hbm[s * 128:(s + 1) * 128, :],
            in_=outb[:, oslot * 1024:(oslot + 1) * 1024]),
            waits=[e_out], inc="os%d" % oslot, inc_n=16)
        st[s].update({"ln4": e_ln4, "out_add": e_out, "ost": e_ost})

    def emit_q2_load(t):
        wq = [ev_qscr[t]]
        if t == 0:
            wq.append(kv_evac)
        elif t >= 2:
            wq.append(q2_last_use[t - 2])
        ev_ql[t] = p.op("sync", lambda e, t=t: e.dma_start(
            out=q2buf(t)[:, :], in_=q_scr[t]),
            waits=wq, inc="ql%d" % (t % 2), inc_n=16)

    def emit_x_load_p2(g, waits):
        slot = g % 4
        ev_x2ld[g] = p.op("sync", lambda e, g=g, slot=slot: e.dma_start(
            out=xsl[:, slot * 1024:(slot + 1) * 1024],
            in_=x_hbm[g * 128:(g + 1) * 128, :]),
            waits=waits, inc="xld%d" % slot, inc_n=16)

    emit_q2_load(0)
    emit_q2_load(1)
    for g in range(NJ):
        emit_x_load_p2(g, [xsl_free.get(g % 4)])
    for u in range(NS + 4):
        if 4 <= u <= NS + 3:
            s3F(u - 4)
        if 1 <= u <= NS:
            s2aT(u - 1)
        if u < NS:
            t, jj = divmod(u, NJ)
            s1a_den(u)
        if 3 <= u <= NS + 2:
            s3T(u - 3)
        if 2 <= u <= NS + 1:
            s2b(u - 2)
        if u < NS:
            s1b_u(u)
            if jj == NJ - 1 and t + 2 < NT:
                emit_q2_load(t + 2)
        if 1 <= u <= NS:
            s2aP(u - 1)

    if dbg:
        p.op("gpsimd", lambda e: e.dma_start(out=d_den, in_=den_sb[:, :]),
             waits=[("ac", p.val("ac"))], inc="dbg", inc_n=16)
        p.op("gpsimd", lambda e: e.dma_start(out=d_z, in_=z_sb[:, :]),
             waits=[("dv", p.val("dv"))], inc="dbg", inc_n=16)

    p.op("sync", lambda e: None,
         waits=[("os0", p.val("os0")), ("os1", p.val("os1"))])

    with nc.Block() as block:
        p.emit(block)
    return nc


_NC_CACHE = None


def kernel(L0_attn_x, L0_attn_Wqkv, L0_attn_Wproj, L0_attn_bproj, L0_ffn_weight):
    global _NC_CACHE
    x = np.asarray(L0_attn_x, dtype=np.float32)
    wqkv = np.asarray(L0_attn_Wqkv, dtype=np.float32)
    wproj = np.asarray(L0_attn_Wproj, dtype=np.float32)
    bproj = np.asarray(L0_attn_bproj, dtype=np.float32).reshape(1, C)
    wffn = np.asarray(L0_ffn_weight, dtype=np.float32)

    wqkvT = np.ascontiguousarray(wqkv.T)
    wprojT = np.ascontiguousarray(wproj.T)
    wffnT = np.ascontiguousarray(wffn.T)

    if _NC_CACHE is None:
        _NC_CACHE = build_program()
    nc = _NC_CACHE

    in_maps = []
    for c in range(8):
        b = c // 4
        r0 = (c % 4) * TOK
        in_maps.append({
            "x": np.ascontiguousarray(x[b, r0:r0 + TOK, :]),
            "wqkvT": wqkvT,
            "wprojT": wprojT,
            "wffnT": wffnT,
            "bproj": bproj,
        })
    res = run_bass_kernel_spmd(nc, in_maps, core_ids=list(range(8)))
    out = np.empty((2, 16384, C), dtype=np.float32)
    for c in range(8):
        b = c // 4
        r0 = (c % 4) * TOK
        out[b, r0:r0 + TOK, :] = res.results[c]["out"]
    return out


# version 25
# speedup vs baseline: 1.0288x; 1.0201x over previous
import numpy as np
import ml_dtypes
import concourse.bass as bass
import concourse.mybir as mybir
from concourse.bass_utils import run_bass_kernel_spmd

F32 = mybir.dt.float32
BF16 = mybir.dt.bfloat16
AF = mybir.ActivationFunctionType
ALU = mybir.AluOpType

ENGINES = ("tensor", "vector", "scalar", "gpsimd", "sync")

C = 1024
TOK = 4096
TW = 512
NT = TOK // TW
NJ = TW // 128
NS = NT * NJ
LN_EPS = 1e-5
ATTN_EPS = 1e-6


class Prog:
    def __init__(self, nc):
        self.nc = nc
        self.sems = {}
        self.counts = {}
        self.stream = {e: [] for e in ENGINES}

    def sem(self, name):
        if name not in self.sems:
            self.sems[name] = self.nc.alloc_semaphore(name=name)
            self.counts[name] = 0
        return self.sems[name]

    def val(self, name):
        return self.counts.get(name, 0)

    def op(self, eng, builder, waits=(), inc=None, inc_n=1):
        wl = {}
        for w in waits:
            if w is None:
                continue
            (sname, v) = w
            if v > 0:
                self.sem(sname)
                wl[sname] = max(wl.get(sname, 0), v)
        if inc is not None:
            self.sem(inc)
            self.counts[inc] += inc_n
        self.stream[eng].append((builder, sorted(wl.items()), inc, inc_n))
        return (inc, self.counts[inc]) if inc is not None else None

    def emit(self, block):
        def mk(eng_name):
            def body(eng):
                for builder, waits, inc, inc_n in self.stream[eng_name]:
                    for (sname, v) in waits:
                        eng.wait_ge(self.sems[sname], v)
                    if builder is not None:
                        inst = builder(eng)
                        if inc is not None:
                            if inst is None:
                                inst = eng.nop()
                            inst.then_inc(self.sems[inc], inc_n)
            return body

        for e in ENGINES:
            if self.stream[e]:
                getattr(block, e)(mk(e))


def build_program(dbg=False):
    nc = bass.Bass("TRN2", target_bir_lowering=False, debug=False)

    x_hbm = nc.dram_tensor("x", [TOK, C], F32, kind="ExternalInput").ap()
    wqkvT = nc.dram_tensor("wqkvT", [C, 3 * C], F32, kind="ExternalInput").ap()
    wprojT = nc.dram_tensor("wprojT", [C, C], F32, kind="ExternalInput").ap()
    wffnT = nc.dram_tensor("wffnT", [C, C], F32, kind="ExternalInput").ap()
    bproj = nc.dram_tensor("bproj", [1, C], F32, kind="ExternalInput").ap()
    out_hbm = nc.dram_tensor("out", [TOK, C], F32, kind="ExternalOutput").ap()

    q_scr = nc.dram_tensor("q_scr", [NT, 128, 4096], BF16)
    cc_in = nc.dram_tensor("cc_in", [128, 8 * 129], F32)
    cc_out = nc.dram_tensor("cc_out", [128, 8 * 129], F32)

    ident = nc.inline_tensor(np.eye(128, dtype=ml_dtypes.bfloat16), name="ident")

    if dbg:
        d_kv = nc.dram_tensor("d_kv", [128, 8 * 129], F32, kind="ExternalOutput").ap()
        d_den = nc.dram_tensor("d_den", [128, 64], F32, kind="ExternalOutput").ap()
        d_z = nc.dram_tensor("d_z", [128, 64], F32, kind="ExternalOutput").ap()

    sb = nc.alloc_sbuf_tensor
    wqkvT_sb = sb("wqkvT_sb", [128, 8 * 3072], BF16)
    wprojT_sb = sb("wprojT_sb", [128, 8 * 1024], BF16)
    wsignT_sb = sb("wsignT_sb", [128, 8 * 1024], BF16)
    bproj_bf = sb("bproj_bf", [128, C], BF16)
    e0_row = sb("e0_row", [128, 128], BF16)
    ident_sb = sb("ident_sb", [128, 128], BF16)
    epsv = sb("epsv", [128, 1], F32)
    epsz = sb("epsz", [128, 1], F32)

    xsl = sb("xsl", [128, 4 * 1024], F32)
    ln_sites = {}
    for nm in ("L1", "L2", "L3", "L4"):
        ln_sites[nm] = {
            "stats": sb("stats" + nm, [128, 12], F32),
            "mvb": sb("mvb" + nm, [128, 2], F32),
            "rstd": sb("rstd" + nm, [128, 1], F32),
            "bvec": sb("bvec" + nm, [128, 1], F32),
            "prev_apply": None,
            "prev_e6": None,
        }

    ln1_tm = sb("ln1_tm", [128, 4096], BF16)
    ln1_fm = sb("ln1_fm", [128, 4096], BF16)
    q_fm = sb("q_fm", [128, 4096], BF16)
    k_tm = sb("k_tm", [128, 4096], BF16)
    v_tm = sb("v_tm", [128, 4 * 1032], BF16)
    kv_acc = sb("kv_acc", [128, 8 * 129], F32)

    kfree = sb("kfree", [128, 1024], F32)
    tmpA = sb("tmpA", [128, 1024], F32)
    tmpB = sb("tmpB", [128, 1024], F32)
    tmpC = sb("tmpC", [128, 1024], F32)
    tmpD = sb("tmpD", [128, 1024], F32)

    q2a = ln1_tm
    q2b = sb("q2b", [128, 4096], BF16)
    kv_sb = sb("kv_sb", [128, 1024], BF16)
    den_rhs = sb("den_rhs", [128, 128], BF16)
    den_sb = sb("den_sb", [128, 64], F32)
    z_sb = sb("z_sb", [128, 64], F32)
    attnz_tm = q_fm
    attnz_fm = ln1_fm
    x2 = sb("x2", [128, 3 * 1024], F32)
    ln3_tm = k_tm
    ln3_fm = v_tm
    outb = sb("outb", [128, 2 * 1024], F32)
    wstage = sb("wstage", [128, 2 * 1024], F32)

    ps_bf_a = nc.alloc_psum_tensor("ps_bf_a", [128, 1024], BF16)
    ps_bf_b = nc.alloc_psum_tensor("ps_bf_b", [128, 1024], BF16)
    ps_f_a = nc.alloc_psum_tensor("ps_f_a", [128, 512], F32)
    ps_f_b = nc.alloc_psum_tensor("ps_f_b", [128, 512], F32)
    ps_big_a = nc.alloc_psum_tensor("ps_big_a", [128, 1024], F32)
    ps_big_b = nc.alloc_psum_tensor("ps_big_b", [128, 1024], F32)

    p = Prog(nc)

    startup_x = []
    p.op("vector", lambda e: e.memset(epsv[:, :], LN_EPS), inc="dv")
    p.op("vector", lambda e: e.memset(epsz[:, :], ATTN_EPS), inc="dv")
    p.op("vector", lambda e: e.memset(kv_acc[:, :], 0.0), inc="dv")
    p.op("vector", lambda e: e.memset(e0_row[:, :], 0.0), inc="dv")
    p.op("vector", lambda e: e.memset(bproj_bf[:, :], 0.0), inc="dv")
    ev_ones = p.op("vector", lambda e: e.memset(e0_row[0:1, :], 1.0),
                   waits=[("dv", p.val("dv"))], inc="dv")
    ev_vones = p.op("vector", lambda e: e.memset(
        bass.AP(v_tm, 128, [[4 * 1032, 128], [1032, 4], [129, 8], [1, 1]]), 1.0),
        inc="dv")
    ev_consts = ("dv", p.val("dv"))
    ev_ident = p.op("sync", lambda e: e.dma_start(out=ident_sb[:], in_=ident.ap()),
                    inc="idl", inc_n=16)
    ev_wqkv_kc = []
    for kc in range(8):
        for half in range(2):
            ev = p.op("gpsimd", lambda e, kc=kc, half=half: e.dma_start(
                out=bass.AP(wqkvT_sb, kc * 3072 + half * 1536,
                            [[24576, 128], [1, 1536]]),
                in_=bass.AP(wqkvT.tensor, kc * 128 * 3072 + half * 1536,
                            [[3072, 128], [1, 1536]])),
                inc="wq%d" % kc, inc_n=16)
        ev_wqkv_kc.append(("wq%d" % kc, 32))
    ev_wqkv = None



    def layer_norm_chunk(site, src3, dst_ap, waits, apply_waits=()):
        sx = ln_sites[site]
        stats, mvb, rstd, bvec = sx["stats"], sx["mvb"], sx["rstd"], sx["bvec"]
        e1 = p.op("vector", lambda e: e.bn_stats(stats[:, 0:6], src3[0]),
                  waits=list(waits), inc="dv")
        e2 = p.op("vector", lambda e: e.bn_stats(stats[:, 6:12], src3[1]),
                  waits=list(waits), inc="dv")
        e3 = p.op("vector", lambda e: e.bn_aggr(mvb[:, :], stats[:, :]),
                  waits=[e1, e2, sx["prev_apply"]], inc="dv")
        e4 = p.op("scalar", lambda e: e.activation(
            out=rstd[:, :], in_=mvb[:, 1:2], func=AF.Ln, bias=epsv[:, :]),
            waits=[e3, sx["prev_e6"]], inc="ac")
        e5 = p.op("scalar", lambda e: e.activation(
            out=rstd[:, :], in_=rstd[:, :], func=AF.Exp, scale=-0.5),
            waits=[e4], inc="ac")
        e6 = p.op("vector", lambda e: e.scalar_tensor_tensor(
            bvec[:, :], mvb[:, 0:1], -1.0, rstd[:, :],
            op0=ALU.mult, op1=ALU.mult), waits=[e5], inc="dv")
        e7 = p.op("scalar", lambda e: e.activation(
            out=dst_ap, in_=src3[2], func=AF.Identity,
            scale=rstd[:, :], bias=bvec[:, :]),
            waits=[e6] + list(apply_waits), inc="ac")
        sx["prev_apply"] = e7
        sx["prev_e6"] = e6
        return e7

    def layer_norm_apply2(site, srcs, dsts, waits, apply_waits=()):
        sx = ln_sites[site]
        stats, mvb, rstd, bvec = sx["stats"], sx["mvb"], sx["rstd"], sx["bvec"]
        e1 = p.op("vector", lambda e: e.bn_stats(stats[:, 0:6], srcs[0]),
                  waits=list(waits), inc="dv")
        e2 = p.op("vector", lambda e: e.bn_stats(stats[:, 6:12], srcs[1]),
                  waits=list(waits), inc="dv")
        e3 = p.op("vector", lambda e: e.bn_aggr(mvb[:, :], stats[:, :]),
                  waits=[e1, e2, sx["prev_apply"]], inc="dv")
        e4 = p.op("scalar", lambda e: e.activation(
            out=rstd[:, :], in_=mvb[:, 1:2], func=AF.Ln, bias=epsv[:, :]),
            waits=[e3, sx["prev_e6"]], inc="ac")
        e5 = p.op("scalar", lambda e: e.activation(
            out=rstd[:, :], in_=rstd[:, :], func=AF.Exp, scale=-0.5),
            waits=[e4], inc="ac")
        e6 = p.op("vector", lambda e: e.scalar_tensor_tensor(
            bvec[:, :], mvb[:, 0:1], -1.0, rstd[:, :],
            op0=ALU.mult, op1=ALU.mult), waits=[e5], inc="dv")
        p.op("scalar", lambda e: e.activation(
            out=dsts[0], in_=srcs[0], func=AF.Identity,
            scale=rstd[:, :], bias=bvec[:, :]),
            waits=[e6] + list(apply_waits), inc="ac")
        e8 = p.op("scalar", lambda e: e.activation(
            out=dsts[1], in_=srcs[1], func=AF.Identity,
            scale=rstd[:, :], bias=bvec[:, :]), inc="ac")
        sx["prev_apply"] = e8
        sx["prev_e6"] = e6
        return e8

    def transpose_block(src, src_off, psum, dst, dst_off, waits):
        pe_ev = None
        for kc in range(8):
            w = waits if kc == 0 else []
            pe_ev = p.op("tensor", lambda e, kc=kc: e.transpose(
                psum[:, kc * 128:(kc + 1) * 128],
                src[:, src_off + kc * 128:src_off + (kc + 1) * 128],
                ident_sb[:, :]), waits=w, inc="pe" if kc == 7 else None)
        rowlen = dst.shape[1]
        ev = p.op("vector", lambda e: e.tensor_copy(
            bass.AP(dst, dst_off, [[rowlen, 128], [512, 8], [1, 128]]),
            psum[:, :]), waits=[pe_ev, ("dv", p.val("dv"))], inc="dv")
        return pe_ev, ev

    def elu_block(psum_ap, tmp_r, tmp_e, dst_ap, waits, dst_waits=()):
        er = p.op("scalar", lambda e: e.activation(
            out=tmp_r, in_=psum_ap, func=AF.Relu, scale=-1.0),
            waits=list(waits) + [("ac", p.val("ac")), ("dv", p.val("dv"))], inc="ac")
        ee = p.op("scalar", lambda e: e.activation(
            out=tmp_e, in_=tmp_r, func=AF.Exp, scale=-1.0), waits=[er], inc="ac")
        es = p.op("vector", lambda e: e.scalar_tensor_tensor(
            dst_ap, psum_ap, 1.0, tmp_e, op0=ALU.add, op1=ALU.max),
            waits=[ee, ("dv", p.val("dv"))] + list(dst_waits), inc="dv")
        return es

    ev_xld = {}
    xsl_free = {}
    ev_qscr = {}
    kv_evac = None
    p1bf = {0: None, 1: None}
    ln1_apply = {}
    t1_done = {}
    prev_p1 = {}
    prev_kv_ev = {}
    deferred_q = {}

    def emit_x_loads(t):
        for jj in range(NJ):
            g = t * NJ + jj
            slot = g % 4
            w = [xsl_free.get(slot)]
            ev_xld[g] = p.op("sync", lambda e, g=g, slot=slot: e.dma_start(
                out=xsl[:, slot * 1024:(slot + 1) * 1024],
                in_=x_hbm[g * 128:(g + 1) * 128, :]),
                waits=w, inc="xld%d" % slot, inc_n=16)

    def emit_ln1_chunk(t, jj):
        g = t * NJ + jj
        slot = g % 4
        xa = xsl[:, slot * 1024:(slot + 1) * 1024]
        aw = [t1_done[t - 1]] if (t > 0 and jj == 0) else []
        e_apply = layer_norm_chunk(
            "L1", (xa[:, 0:512], xa[:, 512:1024], xa),
            ln1_tm[:, jj * 1024:(jj + 1) * 1024],
            [ev_xld[g]], aw)
        ln1_apply.setdefault(t, []).append(e_apply)
        xsl_free[slot] = e_apply

    def emit_ln1(t):
        for jj in range(NJ):
            emit_ln1_chunk(t, jj)

    def kvacc_pair(tp, pp):
        holder = prev_p1[tp]
        ktm_evs = holder["ktm"]
        vtm_evs = holder["vtm"]
        for pr in (2 * pp, 2 * pp + 1):
            psd = ps_f_a if pr % 2 == 0 else ps_f_b
            if pr >= 2:
                bw = [prev_kv_ev[pr - 2]]
            else:
                tq = tp
                while tq >= 0 and not prev_p1[tq]["qelu"]:
                    tq -= 1
                bw = ([prev_p1[tq]["qelu"][6 + (pr % 2)]] if tq >= 0 else [])
            pe_ev = None
            for jj in range(NJ):
                w = (ktm_evs + vtm_evs + bw) if jj == 0 else []
                pe_ev = p.op("tensor", lambda e, pr=pr, jj=jj, psd=psd: e.matmul(
                    psd[:, 0:129],
                    k_tm[:, jj * 1024 + pr * 128: jj * 1024 + (pr + 1) * 128],
                    v_tm[:, jj * 1032 + pr * 129: jj * 1032 + (pr + 1) * 129],
                    start=(jj == 0), stop=(jj == NJ - 1)),
                    waits=w, inc="pe" if jj == NJ - 1 else None)
            ekv = p.op("vector", lambda e, pr=pr, psd=psd: e.tensor_tensor(
                kv_acc[:, pr * 129:(pr + 1) * 129],
                kv_acc[:, pr * 129:(pr + 1) * 129],
                psd[:, 0:129], op=ALU.add),
                waits=[pe_ev, ev_consts], inc="dv")
            prev_kv_ev[pr] = ekv

    def kvacc_group(tp):
        for pp in range(4):
            kvacc_pair(tp, pp)

    emit_x_loads(0)
    emit_ln1(0)


    emit_x_loads(0)
    emit_ln1(0)

    for t in range(NT):
        tr_evs = []
        last_tr = None
        for jj in range(NJ):
            psum = ps_bf_a if jj % 2 == 0 else ps_bf_b
            bw = [p1bf[jj % 2]]
            if t == 0 and jj == 0:
                bw = bw + [ev_ident]
            pe_ev, ev = transpose_block(
                ln1_tm, jj * 1024, psum, ln1_fm, jj * 128,
                [ln1_apply[t][jj]] + bw)
            p1bf[jj % 2] = ev
            tr_evs.append(ev)
            last_tr = pe_ev
        t1_done[t] = last_tr
        ln1fm_ready = tr_evs

        if t > 0:
            kvacc_group(t - 1)

        if t + 1 < NT:
            emit_x_loads(t + 1)

        q_elu_evs = {}
        ktm_evs = []
        vtm_evs = []
        kfree_evs = []

        def q_group(m):
            psq = ps_f_a if m % 2 == 0 else ps_f_b
            if m >= 2:
                bw = [q_elu_evs[m - 2]]
            else:
                bw = [prev_kv_ev.get(6 + m)]
            w0 = ln1fm_ready + bw
            pe_ev = None
            for kc in range(8):
                w = (w0 + [ev_wqkv_kc[kc]]) if t == 0 else (w0 if kc == 0 else [])
                if t == 0 and kc > 0:
                    w = [ev_wqkv_kc[kc]]
                pe_ev = p.op("tensor", lambda e, kc=kc, m=m, psq=psq: e.matmul(
                    psq[:, :],
                    wqkvT_sb[:, kc * 3072 + m * 128: kc * 3072 + (m + 1) * 128],
                    ln1_fm[:, kc * 512:(kc + 1) * 512],
                    start=(kc == 0), stop=(kc == 7)),
                    waits=w, inc="pe" if kc == 7 else None)
            tr_ = (tmpA if m % 2 == 0 else tmpB)[:, 0:512]
            te_ = (tmpC if m % 2 == 0 else tmpD)[:, 0:512]
            dw = [ev_qscr[t - 1]] if (m == 0 and t > 0) else []
            q_elu_evs[m] = elu_block(psq[:, :], tr_, te_,
                                     q_fm[:, m * 512:(m + 1) * 512], [pe_ev], dw)

        def kv_group(jj):
            if jj > 0:
                bw = [kfree_evs[jj - 1], vtm_evs[jj - 1]]
            elif t > 0:
                bw = [prev_p1[t - 1]["kfree"][3], prev_p1[t - 1]["vtm"][3]]
            else:
                bw = []
            pe_ev = None
            for kc in range(8):
                for gidx in range(4):
                    psd = ps_big_a if gidx < 2 else ps_big_b
                    col = (gidx % 2) * 512
                    w = ([tr_evs[jj]] + bw) if (kc == 0 and gidx == 0) else []
                    if t == 0 and gidx == 0:
                        w = w + [ev_wqkv_kc[kc]]
                    pe_ev = p.op("tensor", lambda e, kc=kc, jj=jj, gidx=gidx, psd=psd, col=col: e.matmul(
                        psd[:, col:col + 512],
                        ln1_fm[:, kc * 512 + jj * 128: kc * 512 + (jj + 1) * 128],
                        wqkvT_sb[:, kc * 3072 + 1024 + gidx * 512: kc * 3072 + 1024 + (gidx + 1) * 512],
                        start=(kc == 0), stop=(kc == 7)),
                        waits=w, inc="pe" if (kc == 7 and gidx == 3) else None)
            ek = elu_block(ps_big_a[:, :], tmpA[:, :], tmpC[:, :],
                           k_tm[:, jj * 1024:(jj + 1) * 1024], [pe_ev])
            ktm_evs.append(ek)
            kfree_evs.append(ek)
            evv = p.op("scalar", lambda e, jj=jj: e.copy(
                bass.AP(v_tm, jj * 1032, [[4 * 1032, 128], [129, 8], [1, 128]]),
                ps_big_b[:, :]), waits=[pe_ev, ev_vones, ("ac", p.val("ac"))], inc="ac")
            vtm_evs.append(evv)

        if t < NT - 1:
            for jj in range(NJ):
                kv_group(jj)
                q_group(2 * jj)
                q_group(2 * jj + 1)
                emit_ln1_chunk(t + 1, jj)
            ev_qscr[t] = p.op("gpsimd", lambda e, t=t: e.dma_start(
                out=q_scr[t], in_=q_fm[:, :]),
                waits=[q_elu_evs[7]], inc="qs", inc_n=16)
        else:
            for jj in range(NJ):
                kv_group(jj)
            deferred_q[t] = q_group

        prev_p1[t] = {"ktm": ktm_evs, "vtm": vtm_evs, "qelu": q_elu_evs,
                      "kfree": kfree_evs}

        if t == 0:
            ev_wproj = p.op("gpsimd", lambda e: e.dma_start(
                out=wprojT_sb.ap().rearrange("p (kc o) -> p kc o", o=1024),
                in_=wprojT.rearrange("(kc p) o -> p kc o", p=128)), inc="wpj", inc_n=16)
            ev_bproj = p.op("gpsimd", lambda e: e.dma_start(
                out=bproj_bf[0:1, :], in_=bproj), waits=[ev_ones], inc="bpj", inc_n=16)
            sign_done = {}
            for kc in range(8):
                par = kc % 2
                w = [sign_done[kc - 2]] if kc >= 2 else []
                ev_wst = p.op("sync", lambda e, kc=kc, par=par: e.dma_start(
                    out=wstage[:, par * 1024:(par + 1) * 1024],
                    in_=wffnT[kc * 128:(kc + 1) * 128, :]), waits=w,
                    inc="wst%d" % par, inc_n=16)
                sign_done[kc] = p.op("scalar", lambda e, kc=kc, par=par: e.activation(
                    out=wsignT_sb[:, kc * 1024:(kc + 1) * 1024],
                    in_=wstage[:, par * 1024:(par + 1) * 1024], func=AF.Sign),
                    waits=[ev_wst, ("ac", p.val("ac"))], inc="ac")
            ev_sign = sign_done[7]


    kvacc_group(NT - 1)
    kv_evac = prev_kv_ev[7]
    kv_all_done = [prev_kv_ev[pr] for pr in range(8)]

    e_cc1 = p.op("gpsimd", lambda e: e.dma_start(out=cc_in.ap(), in_=kv_acc[:, :]),
                 waits=[kv_evac], inc="kvio", inc_n=16)
    e_cc2 = p.op("gpsimd", lambda e: e.collective_compute(
        "AllReduce", ALU.add,
        replica_groups=[[0, 1, 2, 3], [4, 5, 6, 7]],
        ins=[cc_in.ap().opt()], outs=[cc_out.ap().opt()]),
        waits=[e_cc1], inc="cc", inc_n=1)
    e_cc3 = p.op("gpsimd", lambda e: e.dma_start(out=kv_acc[:, :], in_=cc_out.ap()),
                 waits=[e_cc2], inc="kvio", inc_n=16)

    qg = deferred_q[NT - 1]
    for m in range(8):
        qg(m)
    lastq = prev_p1[NT - 1]["qelu"]
    ev_qscr[NT - 1] = p.op("gpsimd", lambda e: e.dma_start(
        out=q_scr[NT - 1], in_=q_fm[:, :]),
        waits=[lastq[7]], inc="qs", inc_n=16)

    e_m1 = p.op("vector", lambda e: e.memset(kv_sb[:, :], 0.0),
                waits=[("dv", p.val("dv"))], inc="dv")
    e_m2 = p.op("vector", lambda e: e.memset(den_rhs[:, :], 0.0), inc="dv")
    e_c1 = p.op("vector", lambda e: e.tensor_copy(
        bass.AP(kv_sb, 0, [[1024, 64], [128, 8], [1, 64]]),
        bass.AP(kv_acc, 0, [[8 * 129, 64], [129, 8], [1, 64]])),
        waits=[e_cc3, e_m1], inc="dv")
    e_c2 = p.op("vector", lambda e: e.tensor_copy(
        bass.AP(kv_sb, 64 * 1024 + 64, [[1024, 64], [128, 8], [1, 64]]),
        bass.AP(kv_acc, 64 * 8 * 129 + 64, [[8 * 129, 64], [129, 8], [1, 64]])),
        waits=[e_cc3, e_m1], inc="dv")
    e_c3 = p.op("vector", lambda e: e.tensor_copy(
        bass.AP(den_rhs, 0, [[128, 64], [18, 8], [1, 1]]),
        bass.AP(kv_acc, 128, [[8 * 129, 64], [129, 8], [1, 1]])),
        waits=[e_cc3, e_m2], inc="dv")
    e_c4 = p.op("vector", lambda e: e.tensor_copy(
        bass.AP(den_rhs, 64 * 128 + 1, [[128, 64], [18, 8], [1, 1]]),
        bass.AP(kv_acc, 64 * 8 * 129 + 128, [[8 * 129, 64], [129, 8], [1, 1]])),
        waits=[e_cc3, e_m2], inc="dv")
    kvsb_ready = [e_c1, e_c2]
    denrhs_ready = [e_c3, e_c4]

    if dbg:
        p.op("gpsimd", lambda e: e.dma_start(out=d_kv, in_=kv_acc[:, :]),
             waits=[e_cc3], inc="dbg", inc_n=16)

    ev_ql = {}
    ev_x2ld = {}
    q2buf = lambda t: q2a if t % 2 == 0 else q2b
    q2_last_use = {}

    st = {}

    def s1a_den(s):
        t, jj = divmod(s, NJ)
        q2 = q2buf(t)
        bw = []
        if s >= 1:
            bw.append(st[s - 1]["attnz"])
        pe_ev = None
        for fc in range(8):
            w = (denrhs_ready + [ev_ql[t]] + bw) if fc == 0 else []
            pe_ev = p.op("tensor", lambda e, fc=fc, jj=jj, q2=q2: e.matmul(
                ps_big_a[:, 512:528],
                q2[:, fc * 512 + jj * 128: fc * 512 + (jj + 1) * 128],
                den_rhs[:, fc * 16:(fc + 1) * 16],
                start=(fc == 0), stop=(fc == 7)),
                waits=w, inc="pe" if fc == 7 else None)
        dsw = [st[s - 4]["z"]] if s >= 4 else []
        e_den = p.op("vector", lambda e, jj=jj: e.tensor_scalar(
            den_sb[:, jj * 16:(jj + 1) * 16], ps_big_a[:, 512:528],
            epsz[:, :], None, op0=ALU.add),
            waits=[pe_ev] + dsw, inc="dv")
        e_z = p.op("vector", lambda e, jj=jj: e.reciprocal(
            z_sb[:, jj * 16:(jj + 1) * 16], den_sb[:, jj * 16:(jj + 1) * 16]),
            waits=[e_den], inc="dv")
        st[s] = {"z": e_z, "den_evac": e_den}

    def s1b_u(s):
        t, jj = divmod(s, NJ)
        q2 = q2buf(t)
        pe_ev = None
        for pr in range(8):
            if pr == 0:
                w = kvsb_ready
            elif pr == 4:
                w = [st[s]["den_evac"]]
            else:
                w = []
            pe_ev = p.op("tensor", lambda e, pr=pr, jj=jj, q2=q2: e.matmul(
                ps_big_a[:, pr * 128:(pr + 1) * 128],
                q2[:, pr * 512 + jj * 128: pr * 512 + (jj + 1) * 128],
                kv_sb[:, pr * 128:(pr + 1) * 128],
                start=True, stop=True, skip_group_check=True),
                waits=w, inc="pe" if pr == 7 else None)
        q2_last_use[t] = pe_ev
        aw = [ev_qscr[NT - 1]] if s == 0 else []
        e_attnz = p.op("vector", lambda e, jj=jj: e.tensor_tensor(
            attnz_tm[:, jj * 1024:(jj + 1) * 1024],
            ps_big_a[:, :],
            bass.AP(z_sb, jj * 16, [[64, 128], [1, 16], [0, 64]]),
            op=ALU.mult),
            waits=[pe_ev, st[s]["z"]] + aw, inc="dv")
        st[s]["attnz"] = e_attnz

    def s2aT(s):
        t, jj = divmod(s, NJ)
        bw = [st[s - 1]["afm"]] if s >= 1 else []
        pe_tr, e_afm = transpose_block(attnz_tm, jj * 1024, ps_bf_a,
                                       attnz_fm, jj * 128,
                                       [st[s]["attnz"]] + bw)
        st[s]["afm"] = e_afm

    def s2aP(s):
        t, jj = divmod(s, NJ)
        slot = s % 4
        bw = [st[s - 1]["ln2"]] if s >= 1 else []
        pe_ev = None
        for kc in range(8):
            for gidx in range(2):
                w = ([st[s]["afm"], ev_wproj, ev_ones] + bw) if (kc == 0 and gidx == 0) else []
                pe_ev = p.op("tensor", lambda e, kc=kc, jj=jj, gidx=gidx: e.matmul(
                    ps_big_b[:, gidx * 512:(gidx + 1) * 512],
                    attnz_fm[:, kc * 512 + jj * 128: kc * 512 + (jj + 1) * 128],
                    wprojT_sb[:, kc * 1024 + gidx * 512: kc * 1024 + (gidx + 1) * 512],
                    start=(kc == 0), stop=False),
                    waits=w, inc=None)
        for gidx in range(2):
            pe_ev = p.op("tensor", lambda e, gidx=gidx: e.matmul(
                ps_big_b[:, gidx * 512:(gidx + 1) * 512],
                e0_row[0:1, :],
                bproj_bf[0:1, gidx * 512:(gidx + 1) * 512],
                start=False, stop=True),
                waits=[ev_bproj] if gidx == 0 else [],
                inc="pe" if gidx == 1 else None)
        ltmp = tmpA if s % 2 == 0 else tmpC
        aw = [st[s - 2]["x2"]] if s >= 2 else []
        e_ln2 = layer_norm_chunk(
            "L2", (ps_big_b[:, 0:512], ps_big_b[:, 512:1024], ps_big_b[:, :]),
            ltmp[:, :], [pe_ev], aw)
        xp = s % 3
        bw = [st[s - 3]["out_add"]] if s >= 3 else []
        e_x2 = p.op("gpsimd", lambda e, xp=xp, slot=slot, ltmp=ltmp: e.tensor_tensor(
            x2[:, xp * 1024:(xp + 1) * 1024],
            xsl[:, slot * 1024:(slot + 1) * 1024], ltmp[:, :], op=ALU.add),
            waits=[e_ln2, ev_x2ld[s], ("gp", p.val("gp"))] + bw, inc="gp")
        xsl_free[slot] = e_x2
        if s + 4 < NS:
            emit_x_load_p2(s + 4, [e_x2])
        st[s].update({"ln2": e_ln2, "x2": e_x2})

    def s2b(s):
        t, jj = divmod(s, NJ)
        xp = s % 3
        x2c = x2[:, xp * 1024:(xp + 1) * 1024]
        aw = [st[s - 4]["tln3_pe"]] if s >= 4 else []
        e_ln3 = layer_norm_chunk(
            "L3", (x2c[:, 0:512], x2c[:, 512:1024], x2c),
            ln3_tm[:, jj * 1024:(jj + 1) * 1024], [st[s]["x2"]], aw)
        st[s]["ln3"] = e_ln3

    def s3T(s):
        t, jj = divmod(s, NJ)
        bw = [st[s - 1]["l3fm"]] if s >= 1 else []
        pe_tr, e_l3fm = transpose_block(ln3_tm, jj * 1024, ps_bf_b,
                                        ln3_fm, jj * 128,
                                        [st[s]["ln3"]] + bw)
        st[s].update({"tln3_pe": pe_tr, "l3fm": e_l3fm})

    def s3F(s):
        t, jj = divmod(s, NJ)
        bw = [st[s - 1]["ln4"]] if s >= 1 else []
        pe_ev = None
        for kc in range(8):
            for gidx in range(2):
                psd = ps_f_a if gidx == 0 else ps_f_b
                w = ([st[s]["l3fm"], ev_sign] + bw) if (kc == 0 and gidx == 0) else []
                pe_ev = p.op("tensor", lambda e, kc=kc, jj=jj, gidx=gidx, psd=psd: e.matmul(
                    psd[:, :],
                    ln3_fm[:, kc * 512 + jj * 128: kc * 512 + (jj + 1) * 128],
                    wsignT_sb[:, kc * 1024 + gidx * 512: kc * 1024 + (gidx + 1) * 512],
                    start=(kc == 0), stop=(kc == 7)),
                    waits=w, inc="pe" if (kc == 7 and gidx == 1) else None)
        ltmp = tmpB if s % 2 == 0 else tmpD
        aw = [st[s - 2]["out_add"]] if s >= 2 else []
        e_ln4 = layer_norm_apply2(
            "L4", (ps_f_a[:, :], ps_f_b[:, :]),
            (ltmp[:, 0:512], ltmp[:, 512:1024]), [pe_ev], aw)
        xp = s % 3
        oslot = s % 2
        bw = [st[s - 2]["ost"]] if s >= 2 else []
        e_out = p.op("gpsimd", lambda e, xp=xp, oslot=oslot, ltmp=ltmp: e.tensor_tensor(
            outb[:, oslot * 1024:(oslot + 1) * 1024],
            x2[:, xp * 1024:(xp + 1) * 1024], ltmp[:, :], op=ALU.add),
            waits=[e_ln4, ("gp", p.val("gp"))] + bw, inc="gp")
        e_ost = p.op("sync", lambda e, s=s, oslot=oslot: e.dma_start(
            out=out_hbm[s * 128:(s + 1) * 128, :],
            in_=outb[:, oslot * 1024:(oslot + 1) * 1024]),
            waits=[e_out], inc="os%d" % oslot, inc_n=16)
        st[s].update({"ln4": e_ln4, "out_add": e_out, "ost": e_ost})

    def emit_q2_load(t):
        wq = [ev_qscr[t]]
        if t == 0:
            wq.append(kv_evac)
        elif t >= 2:
            wq.append(q2_last_use[t - 2])
        ev_ql[t] = p.op("sync", lambda e, t=t: e.dma_start(
            out=q2buf(t)[:, :], in_=q_scr[t]),
            waits=wq, inc="ql%d" % (t % 2), inc_n=16)

    def emit_x_load_p2(g, waits):
        slot = g % 4
        ev_x2ld[g] = p.op("sync", lambda e, g=g, slot=slot: e.dma_start(
            out=xsl[:, slot * 1024:(slot + 1) * 1024],
            in_=x_hbm[g * 128:(g + 1) * 128, :]),
            waits=waits, inc="xld%d" % slot, inc_n=16)

    emit_q2_load(0)
    emit_q2_load(1)
    for g in range(NJ):
        emit_x_load_p2(g, [xsl_free.get(g % 4)])
    for u in range(NS + 4):
        if 4 <= u <= NS + 3:
            s3F(u - 4)
        if 1 <= u <= NS:
            s2aT(u - 1)
        if u < NS:
            t, jj = divmod(u, NJ)
            s1a_den(u)
        if 3 <= u <= NS + 2:
            s3T(u - 3)
        if 2 <= u <= NS + 1:
            s2b(u - 2)
        if u < NS:
            s1b_u(u)
            if jj == NJ - 1 and t + 2 < NT:
                emit_q2_load(t + 2)
        if 1 <= u <= NS:
            s2aP(u - 1)

    if dbg:
        p.op("gpsimd", lambda e: e.dma_start(out=d_den, in_=den_sb[:, :]),
             waits=[("ac", p.val("ac"))], inc="dbg", inc_n=16)
        p.op("gpsimd", lambda e: e.dma_start(out=d_z, in_=z_sb[:, :]),
             waits=[("dv", p.val("dv"))], inc="dbg", inc_n=16)

    p.op("sync", lambda e: None,
         waits=[("os0", p.val("os0")), ("os1", p.val("os1"))])

    with nc.Block() as block:
        p.emit(block)
    return nc


_NC_CACHE = None


def kernel(L0_attn_x, L0_attn_Wqkv, L0_attn_Wproj, L0_attn_bproj, L0_ffn_weight):
    global _NC_CACHE
    x = np.asarray(L0_attn_x, dtype=np.float32)
    wqkv = np.asarray(L0_attn_Wqkv, dtype=np.float32)
    wproj = np.asarray(L0_attn_Wproj, dtype=np.float32)
    bproj = np.asarray(L0_attn_bproj, dtype=np.float32).reshape(1, C)
    wffn = np.asarray(L0_ffn_weight, dtype=np.float32)

    wqkvT = np.ascontiguousarray(wqkv.T)
    wprojT = np.ascontiguousarray(wproj.T)
    wffnT = np.ascontiguousarray(wffn.T)

    if _NC_CACHE is None:
        _NC_CACHE = build_program()
    nc = _NC_CACHE

    in_maps = []
    for c in range(8):
        b = c // 4
        r0 = (c % 4) * TOK
        in_maps.append({
            "x": np.ascontiguousarray(x[b, r0:r0 + TOK, :]),
            "wqkvT": wqkvT,
            "wprojT": wprojT,
            "wffnT": wffnT,
            "bproj": bproj,
        })
    res = run_bass_kernel_spmd(nc, in_maps, core_ids=list(range(8)))
    out = np.empty((2, 16384, C), dtype=np.float32)
    for c in range(8):
        b = c // 4
        r0 = (c % 4) * TOK
        out[b, r0:r0 + TOK, :] = res.results[c]["out"]
    return out
